# version 14
# baseline (speedup 1.0000x reference)
import numpy as np
import ml_dtypes
import concourse.bass as bass
import concourse.mybir as mybir
from concourse.bass_utils import run_bass_kernel_spmd

F32 = mybir.dt.float32
BF16 = mybir.dt.bfloat16
I32 = mybir.dt.int32
FP8 = mybir.dt.float8e4
AF = mybir.ActivationFunctionType
ALU = mybir.AluOpType
AX = mybir.AxisListType

B = 2
S = 2048
D = 1024
H = 16
E = 64
DH = 4096
NT = S // 128
NG = S // 512
NC8 = D // 128
NCH = DH // 128
VS = 68
PVS = 72
CSHIFT = 24.0

ENGS = ("sync", "tensor", "vector", "scalar", "gpsimd")
DBG = False


class Sem:
    def __init__(self, h):
        self.h = h
        self.n = 0


class Em:

    def __init__(self):
        self.q = {e: [] for e in ENGS}
        self.waited = {e: {} for e in ENGS}

    def emit(self, eng, fn):
        self.q[eng].append(fn)

    def wait(self, eng, sem, val=None):
        val = sem.n if val is None else val
        if val <= 0:
            return
        key = id(sem)
        if self.waited[eng].get(key, -1) >= val:
            return
        self.waited[eng][key] = val
        h = sem.h
        self.emit(eng, lambda EE, h=h, val=val: EE.wait_ge(h, val))

    def op(self, eng, fn, inc=None, incn=1):
        if inc is None:
            self.emit(eng, fn)
            return None
        inc.n += incn
        h = inc.h

        def g(EE, fn=fn, h=h, incn=incn):
            fn(EE).then_inc(h, incn)

        self.emit(eng, g)
        return inc.n

    def opw(self, eng, fn, inc):
        n = self.op(eng, fn, inc)
        self.waited[eng][id(inc)] = n
        h = inc.h
        self.emit(eng, lambda EE, h=h, n=n: EE.wait_ge(h, n))
        return n

    def dma(self, eng, out, in_, inc, slow=False):
        return self.op(eng, lambda EE, out=out, in_=in_, slow=slow: EE.dma_start(
            out=out, in_=in_, allow_slow_non_contiguous=slow), inc, 16)

    def barrier(self, sems):
        for e in ENGS:
            for s in sems:
                self.wait(e, s)


def build(nc):
    bf = ml_dtypes.bfloat16
    x_d = nc.dram_tensor("x", [B, S, D], F32, kind="ExternalInput").ap()
    wq_d = nc.dram_tensor("wq", [H, D, E], F32, kind="ExternalInput").ap()
    bq_d = nc.dram_tensor("bq", [H, 1, E], F32, kind="ExternalInput").ap()
    wk_d = nc.dram_tensor("wk", [H, D, E], F32, kind="ExternalInput").ap()
    bk_d = nc.dram_tensor("bk", [H, 1, E], F32, kind="ExternalInput").ap()
    wv_d = nc.dram_tensor("wv", [H, D, E], F32, kind="ExternalInput").ap()
    bv_d = nc.dram_tensor("bv", [H, 1, E], F32, kind="ExternalInput").ap()
    wo_d = nc.dram_tensor("wo", [D, D], F32, kind="ExternalInput").ap()
    bo_d = nc.dram_tensor("bo", [D], F32, kind="ExternalInput").ap()
    w1_d = nc.dram_tensor("w1", [DH, D], F32, kind="ExternalInput").ap()
    cw_d = nc.dram_tensor("cw", [DH, 1, 3], F32, kind="ExternalInput").ap()
    cb_d = nc.dram_tensor("cb", [DH], F32, kind="ExternalInput").ap()
    w2_d = nc.dram_tensor("w2", [D, DH], F32, kind="ExternalInput").ap()
    out_d = nc.dram_tensor("out", [B, S, D], F32, kind="ExternalOutput").ap()

    x1_h = nc.dram_tensor("x1h", [B, S, D], F32, kind="Internal").ap()
    xq_h = nc.dram_tensor("xqh", [NC8, 128, B * S], BF16, kind="Internal").ap()
    t1_h = nc.dram_tensor("t1h", [NC8, 128, DH], FP8, kind="Internal").ap()
    t2_h = nc.dram_tensor("t2h", [NCH, 128, D], FP8, kind="Internal").ap()
    wo_h = nc.dram_tensor("woh", [128, NC8 * D], BF16, kind="Internal").ap()

    ident_b = nc.inline_tensor(np.eye(128, dtype=np.float32).astype(bf),
                               name="identb").ap()
    onesrow_b = nc.inline_tensor(np.ones((1, 128), np.float32).astype(bf),
                                 name="onesrb").ap()
    onesrow_f = nc.inline_tensor(np.ones((1, 128), np.float32), name="onesrf").ap()
    onescol_b = nc.inline_tensor(np.ones((128, 1), np.float32).astype(bf),
                                 name="onescb").ap()
    onescol_f = nc.inline_tensor(np.ones((128, 1), np.float32), name="onescf").ap()
    ident_f = nc.inline_tensor(np.eye(128, dtype=np.float32), name="identf").ap()

    em = Em()
    from contextlib import ExitStack

    def tap(name, ap, shape, dt):
        if not DBG or (isinstance(DBG, (set, list)) and name not in DBG):
            return
        dd = nc.dram_tensor("dbg_" + name, list(shape), dt, kind="ExternalOutput").ap()
        for s in ALLSEMS[2:]:
            em.wait("gpsimd", s)
        em.dma("gpsimd", dd, ap, s_st)

    es = ExitStack()

    def sb(name, shape, dt):
        return es.enter_context(nc.sbuf_tensor(name, list(shape), dt))

    s_ld = Sem(es.enter_context(nc.semaphore("s_ld")))
    s_st = Sem(es.enter_context(nc.semaphore("s_st")))
    s_t = Sem(es.enter_context(nc.semaphore("s_t")))
    s_v = Sem(es.enter_context(nc.semaphore("s_v")))
    s_s = Sem(es.enter_context(nc.semaphore("s_s")))
    s_g = Sem(es.enter_context(nc.semaphore("s_g")))
    ALLSEMS = (s_ld, s_st, s_t, s_v, s_s, s_g)

    identb = sb("identb_s", (128, 128), BF16)
    identf = sb("identf_s", (128, 128), F32)
    onesrb = sb("onesrb_s", (1, 128), BF16)
    onesrf = sb("onesrf_s", (1, 128), F32)
    onescb = sb("onescb_s", (128, 1), BF16)
    onescf = sb("onescf_s", (128, 1), F32)
    bq_c = sb("bq_c", (128, H // 2), F32)
    bk_c = sb("bk_c", (128, H // 2), F32)
    cw_c = sb("cw_c", (128, NCH * 3), F32)
    cwb_c = sb("cwb_c", (128, NCH * 3), BF16)
    cb_c = sb("cb_c", (128, NCH), F32)
    w1r_c = sb("w1r_c", (128, 1), F32)
    w1s_c = sb("w1s_c", (128, 1), F32)
    w2r_c = sb("w2r_c", (128, 1), F32)
    w2s_c = sb("w2s_c", (128, 1), F32)
    negC = sb("negC", (128, H), F32)
    mxq = sb("mxq", (1, H), F32)
    mxk = sb("mxk", (1, H), F32)
    t11a = sb("t11a", (1, 4), F32)
    t11b = sb("t11b", (1, 4), F32)
    colp = sb("colp", (128, 40), F32)

    def col(i):
        return colp[:, i:i + 1]

    def ln_chain(par, src, dst, extra_wait=None):
        c0, c1, c2, c3, c4, c5, c6, c7 = (col(par * 8 + k) for k in range(8))
        em.opw("vector", lambda V: V.reduce_sum(c0, src, AX.X), s_v)
        em.wait("scalar", s_v)
        if extra_wait is not None:
            extra_wait()
        em.op("scalar", lambda Sc: Sc.activation(lnsq[:, 0:D], src, AF.Square), s_s)
        em.wait("vector", s_s)
        em.opw("vector", lambda V: V.reduce_sum(c1, lnsq[:, 0:D], AX.X), s_v)
        em.opw("vector", lambda V: V.tensor_scalar(c2, c0, 1.0 / D, None, ALU.mult), s_v)
        em.opw("vector", lambda V: V.tensor_mul(c3, c2, c2), s_v)
        em.opw("vector", lambda V: V.scalar_tensor_tensor(
            c4, c1, 1.0 / D, c3, ALU.mult, ALU.subtract), s_v)
        em.opw("vector", lambda V: V.tensor_scalar(c4, c4, 1e-5, None, ALU.add), s_v)
        em.wait("scalar", s_v)
        em.op("scalar", lambda Sc: Sc.activation(c5, c4, AF.Sqrt), s_s)
        em.wait("vector", s_s)
        em.opw("vector", lambda V: V.reciprocal(c6, c5), s_v)
        em.opw("vector", lambda V: V.scalar_tensor_tensor(
            c7, c2, -1.0, c6, ALU.mult, ALU.mult), s_v)
        em.wait("scalar", s_v)
        em.op("scalar", lambda Sc: Sc.activation(
            dst, src, AF.Identity, bias=c7, scale=c6), s_s)
        return s_s.n

    em.dma("sync", identb[:], ident_b, s_ld)
    em.dma("sync", identf[:], ident_f, s_ld)
    em.dma("sync", onesrb[:], onesrow_b, s_ld)
    em.dma("sync", onesrf[:], onesrow_f, s_ld)
    em.dma("sync", onescb[:], onescol_b, s_ld)
    em.dma("sync", onescf[:], onescol_f, s_ld)
    em.dma("sync", bq_c[:], bq_d.flatten().rearrange("(pr p) -> p pr", p=128), s_ld, slow=True)
    em.dma("sync", bk_c[:], bk_d.flatten().rearrange("(pr p) -> p pr", p=128), s_ld, slow=True)
    em.dma("sync", cw_c[:].rearrange("p (cc k) -> p cc k", k=3), cw_d.rearrange("(cc p) o k -> p cc (o k)", p=128), s_ld, slow=True)
    em.dma("sync", cb_c[:], cb_d.rearrange("(cc p) -> p cc", p=128), s_ld, slow=True)
    pslow_ld = s_ld.n
    em.wait("vector", s_ld, pslow_ld - 4 * 16)

    es_w = ExitStack()
    stg = es_w.enter_context(nc.sbuf_tensor("stg", [128, NC8 * D], F32))
    wo_nat = es_w.enter_context(nc.sbuf_tensor("wo_nat", [128, NC8 * D], BF16))
    wt = es_w.enter_context(nc.sbuf_tensor("wt", [128, DH], F32))
    qt = es_w.enter_context(nc.sbuf_tensor("qt", [128, DH], F32))
    qn = es_w.enter_context(nc.sbuf_tensor("qn", [128, DH], I32))
    qd = es_w.enter_context(nc.sbuf_tensor("qd", [128, DH], F32))
    qp = es_w.enter_context(nc.sbuf_tensor("qp", [128, DH], F32))
    qm = es_w.enter_context(nc.sbuf_tensor("qm", [128, DH], BF16))
    tb = es_w.enter_context(nc.sbuf_tensor("tb", [128, DH], BF16))
    ts8 = es_w.enter_context(nc.sbuf_tensor("ts8", [128, 1024], FP8))
    tsb = es_w.enter_context(nc.sbuf_tensor("tsb", [128, 1024], FP8))
    ptr_w = es_w.enter_context(nc.psum_tensor("ptr_w", [128, 1024], BF16))
    pone = es_w.enter_context(nc.psum_tensor("pone", [128, 512], F32))
    if True:
        Wo_T = es_w.enter_context(nc.sbuf_tensor("Wo_T", [128, NC8 * D], BF16))
        em.dma("scalar", stg[:].rearrange("p (oc i) -> p oc i", oc=NC8),
           wo_d.rearrange("(oc p) i -> p oc i", p=128), s_ld)
        em.wait("vector", s_ld)
        em.op("vector", lambda V: V.tensor_copy(wo_nat[:], stg[:]), s_v)
        em.wait("tensor", s_v)
        for oc in range(NC8):
            for ic in range(NC8):
                em.op("tensor", lambda T, oc=oc, ic=ic: T.transpose(
                    ptr_w[:, ic * 128:(ic + 1) * 128],
                    wo_nat[:, oc * D + ic * 128:oc * D + (ic + 1) * 128],
                    identb[:]), s_t)
            tcnt = s_t.n
            em.wait("vector", s_t, tcnt)
            em.op("vector", lambda V, oc=oc: V.tensor_copy(
                Wo_T[:].rearrange("p (ic o) -> p ic o", ic=NC8)[
                    :, :, oc * 128:(oc + 1) * 128],
                ptr_w[:].rearrange("p (c t) -> p c t", c=NC8)), s_v)
            em.wait("tensor", s_v)

        em.wait("gpsimd", s_v)
        em.dma("gpsimd", wo_h, Wo_T[:], s_st)

        def wprep(w_ap, nrow, scol, rcol, thbm, tstage, nelem):
            ncol = w_ap.shape[1]
            ntile = nrow // 128
            em.op("vector", lambda V: V.memset(col(0), 0.0), s_v)
            for it in range(ntile):
                buf = wt[:, 0:ncol]
                em.wait("scalar", s_v)
                em.dma("scalar", buf, w_ap[it * 128:(it + 1) * 128, :], s_ld)
                em.wait("vector", s_ld)
                em.opw("vector", lambda V, buf=buf: V.reduce_sum(
                    col(1), buf, AX.X, apply_absolute_value=True), s_v)
                em.opw("vector", lambda V: V.tensor_add(col(0), col(0), col(1)), s_v)
            em.wait("tensor", s_v)
            em.op("tensor", lambda T: T.matmul(
                pone[:1, 0:1], colp[:, 0:1], onescf[:, 0:1],
                start=True, stop=True), s_t)
            em.wait("vector", s_t)
            em.opw("vector", lambda V: V.tensor_scalar(
                t11a[:, 0:1], pone[:1, 0:1], 1.0 / nelem, 1e-5, ALU.mult, ALU.max), s_v)
            em.opw("vector", lambda V: V.reciprocal(t11a[:, 1:2], t11a[:, 0:1]), s_v)
            em.wait("tensor", s_v)
            em.op("tensor", lambda T: T.matmul(
                pone[:, 0:1], onesrf[:1, :], t11a[:1, 0:1], start=True, stop=True), s_t)
            em.op("tensor", lambda T: T.matmul(
                pone[:, 1:2], onesrf[:1, :], t11a[:1, 1:2], start=True, stop=True), s_t)
            em.wait("vector", s_t)
            em.op("vector", lambda V: V.tensor_copy(rcol, pone[:, 0:1]), s_v)
            em.op("vector", lambda V: V.tensor_copy(scol, pone[:, 1:2]), s_v)
            em.wait("tensor", s_v)
            for it in range(ntile):
                buf = wt[:, 0:ncol]
                em.wait("scalar", s_v)
                em.dma("scalar", buf, w_ap[it * 128:(it + 1) * 128, :], s_ld)
                em.wait("vector", s_ld)
                T_ = qt[:, :ncol]; N_ = qn[:, :ncol]; Dd = qd[:, :ncol]
                P_ = qp[:, :ncol]; M_ = qm[:, :ncol]; TB = tb[:, :ncol]
                em.wait("vector", s_t)
                em.op("vector", lambda V, buf=buf, T_=T_: V.tensor_scalar(
                    T_, buf, scol, None, ALU.mult), s_v)
                em.op("vector", lambda V, T_=T_, N_=N_: V.tensor_copy(N_, T_), s_v)
                em.op("vector", lambda V, T_=T_, N_=N_, Dd=Dd: V.tensor_sub(Dd, T_, N_), s_v)
                em.op("vector", lambda V, Dd=Dd, N_=N_, P_=P_: V.scalar_tensor_tensor(
                    P_, Dd, 0.5, N_, ALU.is_gt, ALU.add), s_v)
                em.op("vector", lambda V, Dd=Dd, M_=M_: V.tensor_scalar(
                    M_, Dd, -0.5, None, ALU.is_lt), s_v)
                em.op("vector", lambda V, P_=P_, M_=M_: V.tensor_sub(P_, P_, M_), s_v)
                em.op("vector", lambda V, P_=P_, TB=TB: V.tensor_scalar(
                    TB, P_, 1.0, -1.0, ALU.min, ALU.max), s_v)
                vq = s_v.n
                em.wait("tensor", s_v, vq)
                ntr = ncol // 128
                for j0 in range(0, ntr, 8):
                    jn = min(8, ntr - j0)
                    em.wait("tensor", s_v)
                    for j in range(jn):
                        em.op("tensor", lambda T, j0=j0, j=j, TB=TB: T.transpose(
                            ptr_w[:, j * 128:(j + 1) * 128],
                            TB[:, (j0 + j) * 128:(j0 + j + 1) * 128], identb[:]), s_t)
                    tcnt = s_t.n
                    em.wait("vector", s_t, tcnt)
                    em.wait("vector", s_st)
                    em.op("vector", lambda V, jn=jn: V.tensor_copy(
                        tstage[:, 0:jn * 128], ptr_w[:, 0:jn * 128]), s_v)
                    vcnt = s_v.n
                    em.wait("gpsimd", s_v, vcnt)
                    for j in range(jn):
                        em.dma("gpsimd", thbm[j0 + j, :, it * 128:(it + 1) * 128],
                               tstage[:, j * 128:(j + 1) * 128], s_st)

        wprep(w1_d, DH, w1s_c[:, 0:1], w1r_c[:, 0:1], t1_h, ts8, DH * D)
        wprep(w2_d, D, w2s_c[:, 0:1], w2r_c[:, 0:1], t2_h, tsb, DH * D)
        em.barrier(ALLSEMS)
    es_w.close()

    lnsq = sb("lnsq", (128, 2 * D), F32)
    es_a = ExitStack()

    def sba(name, shape, dt):
        return es_a.enter_context(nc.sbuf_tensor(name, list(shape), dt))

    bvrow = sba("bvrow", (1, D), BF16)
    borow = sba("borow", (1, D), BF16)
    Qp = [sba(f"Qp{i}", (128, S), BF16) for i in range(H // 2)]
    Kp = [sba(f"Kp{i}", (128, S), BF16) for i in range(H // 2)]
    Vt = sba("Vt", (128, NT * H * VS), BF16)
    Vt3 = Vt[:].rearrange("p (ct x) -> p ct x", ct=NT)
    em.dma("sync", lnsq[0:1, 0:D], bv_d.flatten()[None, :], s_ld)
    em.wait("vector", s_ld)
    em.op("vector", lambda V: V.tensor_copy(bvrow[:], lnsq[0:1, 0:D]), s_v)
    em.wait("sync", s_v)
    em.dma("sync", lnsq[0:1, 0:D], bo_d[None, :], s_ld)
    em.wait("vector", s_ld)
    em.op("vector", lambda V: V.tensor_copy(borow[:], lnsq[0:1, 0:D]), s_v)
    em.op("vector", lambda V: V.memset(Vt[:], 0.0), s_v)
    em.op("vector", lambda V: V.memset(
        Vt[:].rearrange("p (cth e) -> p cth e", e=VS)[:, :, 64:65], 1.0), s_v)

    def _batch(b):
        es_pw = ExitStack()
        wstg = es_pw.enter_context(nc.sbuf_tensor(f"wstg{b}", [128, NC8 * D // 2], F32))
        wq_s = es_pw.enter_context(nc.sbuf_tensor(f"wq_s{b}", [128, NC8 * D], BF16))
        wk_s = es_pw.enter_context(nc.sbuf_tensor(f"wk_s{b}", [128, NC8 * D], BF16))
        wv_s = es_pw.enter_context(nc.sbuf_tensor(f"wv_s{b}", [128, NC8 * D], BF16))
        if True:
            for wd, wsb, pairs in ((wq_d, wq_s, True), (wk_d, wk_s, True),
                                   (wv_d, wv_s, False)):
                for chalf in range(2):
                    em.wait("sync", s_v)
                    for h in range(H):
                        if pairs:
                            dst = wstg[:].rearrange(
                                "p (c pr he) -> p c pr he", c=NC8 // 2, pr=H // 2)[
                                :, :, h // 2, (h % 2) * 64:(h % 2) * 64 + 64]
                        else:
                            dst = wstg[:].rearrange(
                                "p (c he) -> p c he", c=NC8 // 2)[
                                :, :, h * 64:h * 64 + 64]
                        em.dma("sync", dst,
                               wd[h].rearrange("(c p) e -> p c e", p=128)[
                                   :, chalf * 4:chalf * 4 + 4, :], s_ld)
                    em.wait("vector", s_ld)
                    em.op("vector", lambda V, wsb=wsb, chalf=chalf: V.tensor_copy(
                        wsb[:, chalf * 4 * D:(chalf * 4 + 4) * D], wstg[:]), s_v)

            es_px = ExitStack()
            x_t = es_px.enter_context(nc.sbuf_tensor(f"x_t{b}", [128, 2 * D], F32))
            u_sb = es_px.enter_context(nc.sbuf_tensor(f"u_sb{b}", [128, D], BF16))
            u_fm = es_px.enter_context(nc.sbuf_tensor(f"u_fm{b}", [128, NC8 * 512], BF16))
            qsq = es_px.enter_context(nc.sbuf_tensor(f"qsq{b}", [128, S], BF16))
            ptr_a = es_px.enter_context(nc.psum_tensor(f"ptr_a{b}", [128, 1024], BF16))
            pq0 = es_px.enter_context(nc.psum_tensor(f"pq0{b}", [128, 512]))
            pq1 = es_px.enter_context(nc.psum_tensor(f"pq1{b}", [128, 512]))
            pk0 = es_px.enter_context(nc.psum_tensor(f"pk0{b}", [128, 512]))
            pk1 = es_px.enter_context(nc.psum_tensor(f"pk1{b}", [128, 512]))
            pv0 = es_px.enter_context(nc.psum_tensor(f"pv0{b}", [128, 512]))
            pv1 = es_px.enter_context(nc.psum_tensor(f"pv1{b}", [128, 512]))
            pn = es_px.enter_context(nc.psum_tensor(f"pn{b}", [128, 512]))
            if True:
                ld_q = []

                def xload(b, k):
                    em.wait("sync", s_s)
                    em.dma("sync", x_t[:, (k % 2) * D:(k % 2) * D + D],
                           x_d[b, k * 128:(k + 1) * 128, :], s_ld)
                    ld_q.append(s_ld.n)

                xload(b, 0)
                xload(b, 1)
                for g in range(NG):
                    ufm3 = u_fm[:].rearrange("p (c t) -> p c t", c=NC8)
                    for tt in range(4):
                        kk = g * 4 + tt
                        xt = x_t[:, (kk % 2) * D:(kk % 2) * D + D]
                        ut = u_sb[:, 0:D]
                        em.wait("vector", s_ld, ld_q[kk])
                        em.wait("scalar", s_ld, ld_q[kk])
                        em.wait("scalar", s_t)
                        ln_chain(tt % 2, xt, ut)
                        if b == 0 and g == 0 and tt == 0:
                            tap("xt0", xt, (128, D), F32)
                            tap("colp0", colp[:], (128, 40), F32)
                            tap("usb0", ut, (128, D), BF16)
                        scnt = s_s.n
                        em.wait("tensor", s_s, scnt)
                        for ic in range(NC8):
                            em.op("tensor", lambda T, ic=ic, ut=ut: T.transpose(
                                ptr_a[:, ic * 128:(ic + 1) * 128],
                                ut[:, ic * 128:(ic + 1) * 128], identb[:]), s_t)
                        tcnt = s_t.n
                        em.wait("vector", s_t, tcnt)
                        em.op("vector", lambda V, ufm3=ufm3, tt=tt: V.tensor_copy(
                            ufm3[:, :, tt * 128:(tt + 1) * 128],
                            ptr_a[:].rearrange("p (c t) -> p c t", c=NC8)), s_v)
                        em.wait("tensor", s_v)
                        if kk + 2 < NT:
                            xload(b, kk + 2)
                    em.wait("tensor", s_v)
                    em.wait("scalar", s_ld, pslow_ld)
                    for wsb, pbank, dstl, bcol in (
                            (wq_s, (pq0, pq1), Qp, bq_c),
                            (wk_s, (pk0, pk1), Kp, bk_c)):
                        wsb4 = wsb[:].rearrange(
                            "p (c pr x) -> p c pr x", c=NC8, pr=H // 2)
                        for pr in range(H // 2):
                            pb = pbank[pr % 2]
                            em.wait("tensor", s_s)
                            for c in range(NC8):
                                em.op("tensor", lambda T, pb=pb, wsb4=wsb4, c=c,
                                      pr=pr, ufm3=ufm3: T.matmul(
                                          pb[:], wsb4[:, c, pr, :], ufm3[:, c, :],
                                          start=(c == 0), stop=(c == NC8 - 1)), s_t)
                            tcnt = s_t.n
                            em.wait("scalar", s_t, tcnt)
                            em.op("scalar", lambda Sc, pb=pb, dstl=dstl, pr=pr, g=g,
                                  bcol=bcol: Sc.activation(
                                      dstl[pr][:, g * 512:(g + 1) * 512], pb[:],
                                      AF.Identity, bias=bcol[:, pr:pr + 1], scale=1.0), s_s)
                    wv3 = wv_s[:].rearrange("p (c x) -> p c x", c=NC8)
                    for tt in range(4):
                        ct = g * 4 + tt
                        for half in range(2):
                            pb = (pv0, pv1)[half]
                            em.wait("tensor", s_v)
                            em.op("tensor", lambda T, pb=pb, half=half: T.matmul(
                                pb[:], onesrb[:1, :],
                                bvrow[:1, half * 512:(half + 1) * 512],
                                start=True, stop=False), s_t)
                            for c in range(NC8):
                                em.op("tensor", lambda T, pb=pb, wv3=wv3, c=c,
                                      half=half, ufm3=ufm3, tt=tt: T.matmul(
                                          pb[:], ufm3[:, c, tt * 128:(tt + 1) * 128],
                                          wv3[:, c, half * 512:(half + 1) * 512],
                                          start=False, stop=(c == NC8 - 1)), s_t)
                            tcnt = s_t.n
                            em.wait("vector", s_t, tcnt)
                            em.op("vector", lambda V, pb=pb, ct=ct, half=half:
                                  V.tensor_copy(
                                      Vt3[:, ct, :].rearrange(
                                          "p (h e) -> p h e", e=VS)[
                                          :, half * 8:(half + 1) * 8, 0:64],
                                      pb[:].rearrange("p (h e) -> p h e", e=64)), s_v)

                em.op("vector", lambda V: V.memset(mxq[:], 0.0), s_v)
                em.op("vector", lambda V: V.memset(mxk[:], 0.0), s_v)
                for src, mx in ((Qp, mxq), (Kp, mxk)):
                    for pr in range(H // 2):
                        em.wait("scalar", s_v)
                        em.wait("scalar", s_t)
                        em.op("scalar", lambda Sc, src=src, pr=pr: Sc.activation(
                            qsq[:], src[pr][:], AF.Square), s_s)
                        scnt = s_s.n
                        em.wait("tensor", s_s, scnt)
                        for h2 in range(2):
                            h = pr * 2 + h2
                            for sl in range(4):
                                em.wait("tensor", s_v)
                                em.op("tensor", lambda T, h2=h2, sl=sl: T.matmul(
                                    pn[:1, :], onescb[h2 * 64:h2 * 64 + 64, 0:1],
                                    qsq[h2 * 64:h2 * 64 + 64, sl * 512:(sl + 1) * 512],
                                    start=True, stop=True,
                                    tile_position=(h2 * 64, 0)), s_t)
                                tcnt = s_t.n
                                em.wait("vector", s_t, tcnt)
                                em.opw("vector", lambda V: V.reduce_max(
                                    t11a[:, 0:1], pn[:1, :], AX.X), s_v)
                                em.opw("vector", lambda V, mx=mx, h=h: V.tensor_max(
                                    mx[:, h:h + 1], mx[:, h:h + 1], t11a[:, 0:1]), s_v)
                if b == 0:
                    tap("q0", Qp[0][:], (128, S), BF16)
                    tap("k0", Kp[0][:], (128, S), BF16)
                    tap("vt", Vt[:], (128, NT * H * VS), BF16)
                    tap("ufm", u_fm[:], (128, NC8 * 512), BF16)
                    tap("mxq", mxq[:], (1, H), F32)
                    tap("mxk", mxk[:], (1, H), F32)
                for h in range(H):
                    em.opw("vector", lambda V, h=h: V.tensor_mul(
                        t11a[:, 0:1], mxq[:, h:h + 1], mxk[:, h:h + 1]), s_v)
                    em.wait("scalar", s_v)
                    em.op("scalar", lambda Sc: Sc.activation(
                        t11a[:, 1:2], t11a[:, 0:1], AF.Sqrt), s_s)
                    em.wait("vector", s_s)
                    em.opw("vector", lambda V: V.tensor_scalar(
                        t11b[:, 0:1], t11a[:, 1:2], -0.125, CSHIFT,
                        ALU.mult, ALU.add), s_v)
                    vcnt = s_v.n
                    em.wait("tensor", s_v, vcnt)
                    em.wait("tensor", s_v)
                    em.op("tensor", lambda T: T.matmul(
                        pn[:, 0:1], onesrf[:1, :], t11b[:1, 0:1],
                        start=True, stop=True), s_t)
                    tcnt = s_t.n
                    em.wait("vector", s_t, tcnt)
                    em.op("vector", lambda V, h=h: V.tensor_copy(
                        negC[:, h:h + 1], pn[:, 0:1]), s_v)
            if b == 0:
                tap("negc", negC[:], (128, H), F32)
            es_px.close()
        es_pw.close()
        em.barrier(ALLSEMS)

        es_sc = ExitStack()
        wo_sb = es_sc.enter_context(nc.sbuf_tensor(f"wo_sb{b}", [128, NC8 * D], BF16))
        Pt = es_sc.enter_context(nc.sbuf_tensor(f"Pt{b}", [128, 2 * NT * 512], BF16))
        att_tm = es_sc.enter_context(nc.sbuf_tensor(f"att_tm{b}", [128, 4 * D], BF16))
        att_fm = es_sc.enter_context(nc.sbuf_tensor(f"att_fm{b}", [128, NC8 * 512], BF16))
        y_sb = es_sc.enter_context(nc.sbuf_tensor(f"y_sb{b}", [128, D], F32))
        a_sb = es_sc.enter_context(nc.sbuf_tensor(f"a_sb{b}", [128, D], F32))
        x0_t = es_sc.enter_context(nc.sbuf_tensor(f"x0_t{b}", [128, 2 * D], F32))
        x1_t = es_sc.enter_context(nc.sbuf_tensor(f"x1_t{b}", [128, 2 * D], F32))
        rl_c = es_sc.enter_context(nc.sbuf_tensor(f"rl_c{b}", [128, 2], F32))
        sc0 = es_sc.enter_context(nc.psum_tensor(f"sc0{b}", [128, 512]))
        sc1 = es_sc.enter_context(nc.psum_tensor(f"sc1{b}", [128, 512]))
        sc2 = es_sc.enter_context(nc.psum_tensor(f"sc2{b}", [128, 512]))
        sc3 = es_sc.enter_context(nc.psum_tensor(f"sc3{b}", [128, 512]))
        pvA = es_sc.enter_context(nc.psum_tensor(f"pvA{b}", [128, 2 * PVS]))
        pvB = es_sc.enter_context(nc.psum_tensor(f"pvB{b}", [128, 2 * PVS]))
        ptr_s = es_sc.enter_context(nc.psum_tensor(f"ptr_s{b}", [128, 1024], BF16))
        if True:
            em.dma("sync", wo_sb[:], wo_h, s_ld)
            em.wait("tensor", s_ld)
            scb = (sc0, sc1, sc2, sc3)
            Pt3 = Pt[:].rearrange("p (h c s) -> p h c s", h=2, c=NT)
            ld0_q = []

            def x0load(k):
                em.wait("sync", s_v)
                em.dma("sync", x0_t[:, (k % 2) * D:(k % 2) * D + D],
                       x_d[b, k * 128:(k + 1) * 128, :], s_ld)
                ld0_q.append(s_ld.n)

            x0load(0)
            x0load(1)
            for g in range(NG):
                em.wait("tensor", s_s)
                for pr in range(H // 2):
                    bank_exp = [0, 0, 0, 0]
                    for c in range(NT):
                        for h2 in range(2):
                            j = 2 * h2 + (c % 2)
                            pb = scb[j]
                            em.wait("tensor", s_s, bank_exp[j])
                            em.op("tensor", lambda T, pb=pb, pr=pr, h2=h2, c=c, g=g:
                                  T.matmul(
                                      pb[:],
                                      Kp[pr][h2 * 64:h2 * 64 + 64, c * 128:(c + 1) * 128],
                                      Qp[pr][h2 * 64:h2 * 64 + 64, g * 512:(g + 1) * 512],
                                      start=True, stop=True,
                                      tile_position=(h2 * 64, 0)), s_t)
                            tcnt = s_t.n
                            em.wait("scalar", s_t, tcnt)
                            h = 2 * pr + h2
                            em.op("scalar", lambda Sc, pb=pb, h2=h2, c=c, h=h:
                                  Sc.activation(
                                      Pt3[:, h2, c, :], pb[:], AF.Exp,
                                      bias=negC[:, h:h + 1], scale=0.125), s_s)
                            bank_exp[j] = s_s.n
                    exp_done = s_s.n
                    if b == 0 and g == 0 and pr == 0:
                        tap("pt", Pt[:], (128, 2 * NT * 512), BF16)
                    em.wait("tensor", s_s, exp_done)
                    for h2 in range(2):
                        h = 2 * pr + h2
                        for sl in range(4):
                            pb = (pvA, pvB)[sl % 2]
                            off = (sl // 2) * PVS
                            for c in range(NT):
                                em.op("tensor", lambda T, pb=pb, h2=h2, c=c, sl=sl,
                                      h=h, off=off: T.matmul(
                                          pb[:, off:off + 65],
                                          Pt3[:, h2, c, sl * 128:(sl + 1) * 128],
                                          Vt3[:, c, h * VS:h * VS + 65],
                                          start=(c == 0), stop=(c == NT - 1)), s_t)
                            tcnt = s_t.n
                            em.wait("vector", s_t, tcnt)
                            em.op("vector", lambda V, pb=pb, off=off, sl=sl:
                                  V.reciprocal(rl_c[:, sl % 2:sl % 2 + 1],
                                               pb[:, off + 64:off + 65]), s_v)
                            vcnt = s_v.n
                            em.wait("scalar", s_v, vcnt)
                            em.op("scalar", lambda Sc, pb=pb, off=off, sl=sl, h=h:
                                  Sc.activation(
                                      att_tm[:, sl * D + h * 64:sl * D + h * 64 + 64],
                                      pb[:, off:off + 64], AF.Identity,
                                      scale=rl_c[:, sl % 2:sl % 2 + 1]), s_s)
                            if sl >= 1:
                                em.wait("tensor", s_s)
                    em.wait("tensor", s_s)
                if b == 0 and g == 0:
                    tap("atttm", att_tm[:], (128, 4 * D), BF16)
                for sl in range(4):
                    for ic in range(NC8):
                        em.op("tensor", lambda T, sl=sl, ic=ic: T.transpose(
                            ptr_s[:, ic * 128:(ic + 1) * 128],
                            att_tm[:, sl * D + ic * 128:sl * D + (ic + 1) * 128],
                            identb[:]), s_t)
                    tcnt = s_t.n
                    em.wait("vector", s_t, tcnt)
                    em.op("vector", lambda V, sl=sl: V.tensor_copy(
                        att_fm[:].rearrange("p (c t) -> p c t", c=NC8)[
                            :, :, sl * 128:(sl + 1) * 128],
                        ptr_s[:].rearrange("p (c t) -> p c t", c=NC8)), s_v)
                    vcnt = s_v.n
                    em.wait("tensor", s_v, vcnt)
                    for half in range(2):
                        pb = (sc0, sc1)[half]
                        em.wait("tensor", s_s)
                        em.op("tensor", lambda T, pb=pb, half=half: T.matmul(
                            pb[:], onesrb[:1, :],
                            borow[:1, half * 512:(half + 1) * 512],
                            start=True, stop=False), s_t)
                        for ic in range(NC8):
                            em.op("tensor", lambda T, pb=pb, ic=ic, half=half, sl=sl:
                                  T.matmul(
                                      pb[:],
                                      att_fm[:, ic * 512 + sl * 128:ic * 512 + (sl + 1) * 128],
                                      wo_sb[:, ic * D + half * 512:ic * D + (half + 1) * 512],
                                      start=False, stop=(ic == NC8 - 1)), s_t)
                        tcnt = s_t.n
                        em.wait("scalar", s_t, tcnt)
                        em.wait("scalar", s_v)
                        em.op("scalar", lambda Sc, pb=pb, half=half: Sc.activation(
                            y_sb[:, half * 512:(half + 1) * 512], pb[:], AF.Identity), s_s)
                    if b == 0 and g == 0 and sl == 0:
                        tap("ysb", y_sb[:], (128, D), F32)
                    em.wait("vector", s_s)
                    em.wait("scalar", s_v)
                    ln_chain(sl % 2, y_sb[:], a_sb[:])
                    em.wait("vector", s_s)
                    kk0 = g * 4 + sl
                    em.wait("vector", s_ld, ld0_q[kk0])
                    x0s = x0_t[:, (kk0 % 2) * D:(kk0 % 2) * D + D]
                    x1s = x1_t[:, (kk0 % 2) * D:(kk0 % 2) * D + D]
                    if sl >= 2:
                        em.wait("vector", s_st, s_st.n - 16)
                    em.op("vector", lambda V, x0s=x0s, x1s=x1s: V.tensor_add(
                        x1s, a_sb[:], x0s), s_v)
                    vcnt = s_v.n
                    em.wait("gpsimd", s_v, vcnt)
                    em.dma("gpsimd",
                           x1_h[b, (g * 4 + sl) * 128:(g * 4 + sl + 1) * 128, :],
                           x1s, s_st)
                    if kk0 + 2 < NT:
                        x0load(kk0 + 2)
        es_sc.close()
        em.barrier(ALLSEMS)

    for b in range(B):
        _batch(b)

    es_a.close()

    es_ff = ExitStack()
    rs1row = es_ff.enter_context(nc.sbuf_tensor("rs1row", [1, B * S], F32))
    import os as _os
    if _os.environ.get("KDBG"):
        print("sbuf before F0:", nc.sbuf_bytes_remaining)
    es_f0 = ExitStack()
    f0x = es_f0.enter_context(nc.sbuf_tensor("f0x", [128, 2 * D], F32))
    u2 = es_f0.enter_context(nc.sbuf_tensor("u2", [128, D], F32))
    qtF = es_f0.enter_context(nc.sbuf_tensor("qtF", [128, D], F32))
    qnF = es_f0.enter_context(nc.sbuf_tensor("qnF", [128, D], I32))
    qdF = es_f0.enter_context(nc.sbuf_tensor("qdF", [128, D], F32))
    qpF = es_f0.enter_context(nc.sbuf_tensor("qpF", [128, D], F32))
    qmF = es_f0.enter_context(nc.sbuf_tensor("qmF", [128, D], F32))
    xhat = es_f0.enter_context(nc.sbuf_tensor("xhat", [128, D], BF16))
    xst = es_f0.enter_context(nc.sbuf_tensor("xst", [128, 2 * D], BF16))
    ptr_f = es_f0.enter_context(nc.psum_tensor("ptr_f", [128, 1024], BF16))
    ptr_r = es_f0.enter_context(nc.psum_tensor("ptr_r", [128, 512], F32))
    if True:
        for t in range(B * NT):
            bb, ss = t // NT, (t % NT) * 128
            xb = f0x[:, (t % 2) * D:(t % 2) * D + D]
            em.wait("sync", s_v)
            em.dma("sync", xb, x1_h[bb, ss:ss + 128, :], s_ld)
            em.wait("vector", s_ld)
            em.wait("scalar", s_ld)
            em.wait("scalar", s_v)
            ln_chain(t % 2, xb, u2[:])
            em.wait("vector", s_s)
            c8, c9, c10, c11 = col(16 + (t % 2) * 4), col(17 + (t % 2) * 4), \
                col(18 + (t % 2) * 4), col(19 + (t % 2) * 4)
            em.opw("vector", lambda V, c8=c8: V.reduce_max(
                c8, u2[:], AX.X, apply_absolute_value=True), s_v)
            em.opw("vector", lambda V, c8=c8: V.tensor_scalar(
                c8, c8, 1e-5, None, ALU.max), s_v)
            em.opw("vector", lambda V, c8=c8, c9=c9: V.reciprocal(c9, c8), s_v)
            em.opw("vector", lambda V, c9=c9, c10=c10: V.tensor_scalar(
                c10, c9, 127.0, None, ALU.mult), s_v)
            em.opw("vector", lambda V, c8=c8, c11=c11: V.tensor_scalar(
                c11, c8, 1.0 / 127.0, None, ALU.mult), s_v)
            em.op("vector", lambda V, c10=c10: V.tensor_scalar(
                qtF[:], u2[:], c10, None, ALU.mult), s_v)
            em.op("vector", lambda V: V.tensor_copy(qnF[:], qtF[:]), s_v)
            em.op("vector", lambda V: V.tensor_sub(qdF[:], qtF[:], qnF[:]), s_v)
            em.op("vector", lambda V: V.scalar_tensor_tensor(
                qpF[:], qdF[:], 0.5, qnF[:], ALU.is_gt, ALU.add), s_v)
            em.op("vector", lambda V: V.scalar_tensor_tensor(
                qpF[:], qdF[:], 1.5, qpF[:], ALU.is_gt, ALU.add), s_v)
            em.op("vector", lambda V: V.tensor_scalar(
                qmF[:], qdF[:], -0.5, None, ALU.is_lt), s_v)
            em.op("vector", lambda V: V.scalar_tensor_tensor(
                qmF[:], qdF[:], -1.5, qmF[:], ALU.is_lt, ALU.add), s_v)
            em.op("vector", lambda V: V.tensor_sub(qpF[:], qpF[:], qmF[:]), s_v)
            em.op("vector", lambda V: V.tensor_copy(xhat[:], qpF[:]), s_v)
            vcnt = s_v.n
            em.wait("tensor", s_v, vcnt)
            em.op("tensor", lambda T, c11=c11: T.transpose(
                ptr_r[:1, 0:128], c11, identf[:]), s_t)
            tcnt = s_t.n
            em.wait("vector", s_t, tcnt)
            em.op("vector", lambda V, bb=bb, ss=ss: V.tensor_copy(
                rs1row[:1, bb * S + ss:bb * S + ss + 128], ptr_r[:1, 0:128]), s_v)
            em.wait("tensor", s_v)
            for ic in range(NC8):
                em.op("tensor", lambda T, ic=ic: T.transpose(
                    ptr_f[:, ic * 128:(ic + 1) * 128],
                    xhat[:, ic * 128:(ic + 1) * 128], identb[:]), s_t)
            tcnt = s_t.n
            em.wait("vector", s_t, tcnt)
            if t >= 2:
                em.wait("vector", s_st, s_st.n - 8 * 16)
            xs = xst[:, (t % 2) * D:(t % 2) * D + D]
            em.op("vector", lambda V, xs=xs: V.tensor_copy(xs, ptr_f[:]), s_v)
            vcnt = s_v.n
            em.wait("tensor", s_v)
            em.wait("gpsimd", s_v, vcnt)
            tok0 = bb * S + ss
            for ic in range(NC8):
                em.dma("gpsimd", xq_h[ic, :, tok0:tok0 + 128],
                       xs[:, ic * 128:(ic + 1) * 128], s_st)
        em.barrier(ALLSEMS)
    es_f0.close()

    if _os.environ.get("KDBG"):
        print("sbuf before F1:", nc.sbuf_bytes_remaining)
    es_f1 = ExitStack()
    TW = 256
    TWW = TW + 2
    NTF = (B * S) // TW
    t1s = es_f1.enter_context(nc.sbuf_tensor("t1s", [128, NC8 * DH], FP8))
    t2s = es_f1.enter_context(nc.sbuf_tensor("t2s", [128, NCH * D], FP8))
    xw = es_f1.enter_context(nc.sbuf_tensor("xw", [128, 2 * NC8 * TWW], BF16))
    sy2 = es_f1.enter_context(nc.sbuf_tensor("sy2", [128, 2 * NCH * TW], BF16))
    hsc = es_f1.enter_context(nc.sbuf_tensor("hsc", [128, NCH * TWW], BF16))
    rs1b = es_f1.enter_context(nc.sbuf_tensor("rs1b", [128, 2 * TWW], BF16))
    y0s = es_f1.enter_context(nc.sbuf_tensor("y0s", [128, 2 * TW], BF16))
    y1s = es_f1.enter_context(nc.sbuf_tensor("y1s", [128, 2 * TW], BF16))
    mxp = es_f1.enter_context(nc.sbuf_tensor("mxp", [128, 2 * TW], BF16))
    mxn = es_f1.enter_context(nc.sbuf_tensor("mxn", [128, 2 * TW], BF16))
    rmx = es_f1.enter_context(nc.sbuf_tensor("rmx", [128, 2 * TW], BF16))
    rmn = es_f1.enter_context(nc.sbuf_tensor("rmn", [128, 2 * TW], BF16))
    s2row = es_f1.enter_context(nc.sbuf_tensor("s2row", [1, TW], F32))
    s2b = es_f1.enter_context(nc.sbuf_tensor("s2b", [128, TW], F32))
    tqP = [es_f1.enter_context(nc.sbuf_tensor(f"tqP{k}", [128, 2 * TW], F32))
           for k in range(2)]
    nqP = [es_f1.enter_context(nc.sbuf_tensor(f"nqP{k}", [128, 2 * TW], I32))
           for k in range(2)]
    dqP = [es_f1.enter_context(nc.sbuf_tensor(f"dqP{k}", [128, 2 * TW], BF16))
           for k in range(2)]
    eaP = [es_f1.enter_context(nc.sbuf_tensor(f"eaP{k}", [128, 2 * TW], BF16))
           for k in range(2)]
    s2bw = es_f1.enter_context(nc.sbuf_tensor("s2bw", [128, 2 * TW], F32))
    amx = es_f1.enter_context(nc.sbuf_tensor("amx", [128, 4], F32))
    rc2 = es_f1.enter_context(nc.sbuf_tensor("rc2", [128, 4], F32))
    o2 = es_f1.enter_context(nc.sbuf_tensor("o2", [128, 4 * D], BF16))
    aF = es_f1.enter_context(nc.sbuf_tensor("aF", [128, 2 * D], BF16))
    x1F = es_f1.enter_context(nc.sbuf_tensor("x1F", [128, 2 * D], F32))
    xoF = es_f1.enter_context(nc.sbuf_tensor("xoF", [128, 2 * D], F32))
    fAs = [es_f1.enter_context(nc.psum_tensor(f"fA{k}", [128, TWW])) for k in range(2)]
    f2s = [es_f1.enter_context(nc.psum_tensor(f"f2{k}", [128, 512])) for k in range(4)]
    pbc = es_f1.enter_context(nc.psum_tensor("pbc", [128, TWW]))
    ptr_q = es_f1.enter_context(nc.psum_tensor("ptr_q", [128, TW], BF16))
    if True:
        t1v = t1s[:].rearrange("p (c x) -> p c x", c=NC8)
        t2v = t2s[:].rearrange("p (c x) -> p c x", c=NCH)
        xwv = [xw[:, j * NC8 * TWW:(j + 1) * NC8 * TWW].rearrange(
            "p (c w) -> p c w", c=NC8) for j in range(2)]
        sy2v = [sy2[:, j * NCH * TW:(j + 1) * NCH * TW].rearrange(
            "p (c x) -> p c x", c=NCH) for j in range(2)]

        em.wait("vector", s_ld, pslow_ld)
        em.wait("scalar", s_ld, pslow_ld)
        em.op("vector", lambda V: V.tensor_copy(cwb_c[:], cw_c[:]), s_v)
        for c in range(NC8):
            em.dma("sync", t1v[:, c, :], t1_h[c], s_ld)
        for c in range(NCH):
            em.dma("sync", t2v[:, c, :], t2_h[c], s_ld)
        em.wait("tensor", s_ld)

        ld_xw = [0] * NTF
        ffn1_done = [0] * NTF
        ffn2_done = [0] * NTF
        o2_rdy = [0] * NTF
        o2_vread = [0, 0]
        tp_done = [0] * NTF
        q2_all = [0] * NTF
        rs1_rdy = [0] * NTF
        ds_last = [0] * NTF
        minmax_done = [0] * NTF
        hsc_y1v = [0] * NCH
        hsc_desc = [0] * NCH
        fa_evac = [0, 0]
        f2_mm = [0, 0, 0, 0]
        f2_evac = [0, 0, 0, 0]
        y0_read = [0, 0]
        y1_read = [0, 0]
        g_done = [0, 0]
        col_read = [0, 0]
        aF_read = [0, 0]
        x1_read = [0, 0]
        xo_dma = [0, 0]
        ptrq_read = [0]
        pbc_free = [0]
        ln_prev = [0, 0]

        def _edges(j):
            s0j = (j * TW) % S
            lo = 1 if s0j == 0 else 0
            hi = TWW - 1 if s0j + TW == S else TWW
            return lo, hi

        def emit_xw_load(j):
            jp = j % 2
            Wj = xwv[jp]
            lo, hi = _edges(j)
            tok0 = j * TW
            if j >= 2:
                em.wait("sync", s_t, ffn1_done[j - 2])
            for c in range(NC8):
                em.dma("sync", Wj[:, c, lo:hi],
                       xq_h[c, :, tok0 - 1 + lo:tok0 - 1 + hi], s_ld)
            ld_xw[j] = s_ld.n

        def build_rs1b(j):
            jp = j % 2
            lo, hi = _edges(j)
            tok0 = j * TW
            sl = rs1b[:, jp * TWW:jp * TWW + TWW]
            if j >= 2:
                em.wait("vector", s_g, ds_last[j - 2])
            if lo == 1:
                em.op("vector", lambda V, sl=sl: V.memset(sl[:, 0:1], 1.0), s_v)
            if hi == TWW - 1:
                em.op("vector", lambda V, sl=sl: V.memset(
                    sl[:, TWW - 1:TWW], 1.0), s_v)
            em.wait("tensor", s_v, pbc_free[0])
            em.op("tensor", lambda T, tok0=tok0, lo=lo, hi=hi: T.matmul(
                pbc[:, 0:hi - lo], onesrf[:1, :],
                rs1row[:1, tok0 - 1 + lo:tok0 - 1 + hi],
                start=True, stop=True), s_t)
            tcnt = s_t.n
            em.wait("vector", s_t, tcnt)
            em.op("vector", lambda V, sl=sl, lo=lo, hi=hi: V.tensor_copy(
                sl[:, lo:hi], pbc[:, 0:hi - lo]), s_v)
            pbc_free[0] = s_v.n
            rs1_rdy[j] = s_v.n

        def stage_A1(i):
            ip = i % 2
            W = xwv[ip]
            SY = sy2v[ip]
            mxps = mxp[:, ip * TW:ip * TW + TW]
            mxns = mxn[:, ip * TW:ip * TW + TW]
            rsl = rs1b[:, ip * TWW:ip * TWW + TWW]
            lo, hi = _edges(i)
            if i >= 2:
                em.wait("vector", s_t, tp_done[i - 2])
            em.op("vector", lambda V: V.memset(rmx[:], -1e30), s_v)
            em.op("vector", lambda V: V.memset(rmn[:], 1e30), s_v)
            edge_done = 0
            em.wait("vector", s_ld, ld_xw[i])
            if lo == 1:
                em.op("vector", lambda V, W=W: V.memset(W[:, :, 0:1], 0.0), s_v)
                edge_done = s_v.n
            if hi == TWW - 1:
                em.op("vector", lambda V, W=W: V.memset(
                    W[:, :, TWW - 1:TWW], 0.0), s_v)
                edge_done = s_v.n
            em.wait("tensor", s_ld, ld_xw[i])
            if edge_done:
                em.wait("tensor", s_v, edge_done)
            for cc in range(NCH):
                pA = fAs[cc % 2]
                hv = hsc[:, cc * TWW:(cc + 1) * TWW]
                em.wait("tensor", s_s, fa_evac[cc % 2])
                for c in range(NC8):
                    em.op("tensor", lambda T, pA=pA, c=c, cc=cc, W=W: T.matmul(
                        pA[:], t1v[:, c, cc * 128:(cc + 1) * 128], W[:, c, :],
                        start=(c == 0), stop=(c == NC8 - 1)), s_t)
                mm_cnt = s_t.n
                em.wait("scalar", s_t, mm_cnt)
                em.wait("scalar", s_v, hsc_y1v[cc])
                em.wait("scalar", s_g, hsc_desc[cc])
                em.op("scalar", lambda Sc, pA=pA, hv=hv: Sc.activation(
                    hv, pA[:], AF.Identity, scale=w1r_c[:, 0:1]), s_s)
                fa_evac[cc % 2] = s_s.n
                em.wait("gpsimd", s_s, fa_evac[cc % 2])
                em.wait("gpsimd", s_v, rs1_rdy[i])
                em.op("gpsimd", lambda G, hv=hv, rsl=rsl: G.tensor_mul(
                    hv, hv, rsl), s_g)
                hsc_desc[cc] = s_g.n
            ffn1_done[i] = s_t.n
            ds_last[i] = s_g.n
            if i + 1 < NTF:
                emit_xw_load(i + 1)
                build_rs1b(i + 1)

        def stage_A2(i):
            ip = i % 2
            SY = sy2v[ip]
            mxps = mxp[:, ip * TW:ip * TW + TW]
            mxns = mxn[:, ip * TW:ip * TW + TW]
            if i >= 2:
                em.wait("scalar", s_t, ffn2_done[i - 2])
            def y0_emit(cc):
                hv = hsc[:, cc * TWW:(cc + 1) * TWW]
                y0sl = y0s[:, (cc % 2) * TW:(cc % 2) * TW + TW]
                em.wait("scalar", s_g, hsc_desc[cc])
                em.wait("scalar", s_v, y0_read[cc % 2])
                em.op("scalar", lambda Sc, hv=hv, cc=cc, y0sl=y0sl: Sc.activation(
                    y0sl, hv[:, 1:TW + 1], AF.Identity,
                    bias=cb_c[:, cc:cc + 1],
                    scale=cw_c[:, cc * 3 + 1:cc * 3 + 2]), s_s)
                return s_s.n

            y0_cnt = [0] * NCH
            y0_cnt[0] = y0_emit(0)
            for cc in range(NCH):
                hv = hsc[:, cc * TWW:(cc + 1) * TWW]
                y1sl = y1s[:, (cc % 2) * TW:(cc % 2) * TW + TW]
                em.wait("vector", s_s, y0_cnt[cc])
                em.wait("vector", s_s, y1_read[cc % 2])
                em.op("vector", lambda V, hv=hv, cc=cc, y1sl=y1sl:
                      V.scalar_tensor_tensor(
                          y1sl, hv[:, 0:TW], cwb_c[:, cc * 3:cc * 3 + 1],
                          y0s[:, (cc % 2) * TW:(cc % 2) * TW + TW],
                          ALU.mult, ALU.add), s_v)
                y0_read[cc % 2] = s_v.n
                em.op("vector", lambda V, hv=hv, cc=cc, y1sl=y1sl:
                      V.scalar_tensor_tensor(
                          y1sl, hv[:, 2:TW + 2], cwb_c[:, cc * 3 + 2:cc * 3 + 3],
                          y1sl, ALU.mult, ALU.add), s_v)
                hsc_y1v[cc] = s_v.n
                y1b_cnt = s_v.n
                if cc + 1 < NCH:
                    y0_cnt[cc + 1] = y0_emit(cc + 1)
                em.wait("scalar", s_v, y1b_cnt)
                em.op("scalar", lambda Sc, SY=SY, cc=cc, y1sl=y1sl: Sc.activation(
                    SY[:, cc, :], y1sl, AF.Silu), s_s)
                y1_read[cc % 2] = s_s.n
                silu_cnt = s_s.n
                if cc % 2 == 1:
                    SYp = sy2[:, ip * NCH * TW + (cc - 1) * TW:
                              ip * NCH * TW + (cc + 1) * TW]
                    em.wait("vector", s_s, silu_cnt)
                    em.op("vector", lambda V, SYp=SYp: V.tensor_max(
                        rmx[:], rmx[:], SYp), s_v)
                    em.op("vector", lambda V, SYp=SYp: V.tensor_tensor(
                        rmn[:], rmn[:], SYp, ALU.min), s_v)
            mm_g = s_g.n
            rx3 = rmx[:].rearrange("p (l x) -> p l x", l=2)
            rn3 = rmn[:].rearrange("p (l x) -> p l x", l=2)
            em.op("vector", lambda V, mxps=mxps: V.tensor_tensor(
                mxps, rx3[:, 0, :], rx3[:, 1, :], ALU.max), s_v)
            em.op("vector", lambda V, mxns=mxns: V.tensor_tensor(
                mxns, rn3[:, 0, :], rn3[:, 1, :], ALU.min), s_v)
            minmax_done[i] = s_v.n

        def stage_B(i):
            ip = i % 2
            mxps = mxp[:, ip * TW:ip * TW + TW]
            mxns = mxn[:, ip * TW:ip * TW + TW]
            axs = amx[:, ip * 2:ip * 2 + 2]
            rcs = rc2[:, ip * 2:ip * 2 + 2]
            em.opw("vector", lambda V, mxns=mxns: V.tensor_scalar(
                mxns, mxns, -1.0, None, ALU.mult), s_v)
            em.opw("vector", lambda V, mxps=mxps, mxns=mxns: V.tensor_max(
                mxps, mxps, mxns), s_v)
            vcnt = s_v.n
            em.wait("tensor", s_v, vcnt)
            em.wait("tensor", s_v, ptrq_read[0])
            for t in range(2):
                em.op("tensor", lambda T, t=t, mxps=mxps: T.transpose(
                    ptr_q[:, t * 128:(t + 1) * 128],
                    mxps[:, t * 128:(t + 1) * 128], identb[:]), s_t)
            tp_done[i] = s_t.n
            em.wait("vector", s_t, tp_done[i])
            for t in range(2):
                em.opw("vector", lambda V, t=t, axs=axs: V.reduce_max(
                    axs[:, t:t + 1], ptr_q[:, t * 128:(t + 1) * 128], AX.X), s_v)
            ptrq_read[0] = s_v.n
            em.opw("vector", lambda V, axs=axs: V.tensor_scalar(
                axs, axs, 1e-5, None, ALU.max), s_v)
            em.opw("vector", lambda V, axs=axs, rcs=rcs: V.reciprocal(
                rcs, axs), s_v)
            em.opw("vector", lambda V, rcs=rcs: V.tensor_scalar(
                rcs, rcs, 127.0, None, ALU.mult), s_v)
            vcnt = s_v.n
            em.wait("tensor", s_v, vcnt)
            em.wait("tensor", s_v, pbc_free[0])
            for t in range(2):
                em.op("tensor", lambda T, t=t, rcs=rcs: T.transpose(
                    pbc[:1, t * 128:(t + 1) * 128], rcs[:, t:t + 1],
                    identf[:]), s_t)
            tcnt = s_t.n
            em.wait("vector", s_t, tcnt)
            em.op("vector", lambda V: V.tensor_copy(
                s2row[:1, :], pbc[:1, 0:TW]), s_v)
            srow_cnt = s_v.n
            em.wait("tensor", s_v, srow_cnt)
            em.op("tensor", lambda T: T.matmul(
                pbc[:, 0:TW], onesrf[:1, :], s2row[:1, :],
                start=True, stop=True), s_t)
            tcnt = s_t.n
            em.wait("vector", s_t, tcnt)
            em.op("vector", lambda V: V.tensor_copy(s2b[:], pbc[:, 0:TW]), s_v)
            pbc_free[0] = s_v.n

        def stage_C(i):
            ip = i % 2
            em.op("vector", lambda V: V.tensor_copy(s2bw[:, 0:TW], s2b[:]), s_v)
            em.op("vector", lambda V: V.tensor_copy(s2bw[:, TW:2 * TW], s2b[:]), s_v)
            for cp in range(NCH // 2):
                par = cp % 2
                tq = tqP[par][:]; nq = nqP[par][:]
                dq = dqP[par][:]; ea = eaP[par][:]
                SYp = sy2[:, ip * NCH * TW + cp * 2 * TW:
                          ip * NCH * TW + (cp + 1) * 2 * TW]
                em.op("vector", lambda V, tq=tq, SYp=SYp: V.tensor_mul(
                    tq, SYp, s2bw[:]), s_v)
                em.op("vector", lambda V, tq=tq, nq=nq: V.tensor_copy(nq, tq), s_v)
                em.op("vector", lambda V, tq=tq, nq=nq, dq=dq: V.tensor_sub(
                    dq, tq, nq), s_v)
                em.op("vector", lambda V, dq=dq, ea=ea: V.tensor_scalar(
                    ea, dq, -0.5, None, ALU.is_lt), s_v)
                for thr, op0 in ((0.5, ALU.is_gt), (-1.5, ALU.is_lt),
                                 (1.5, ALU.is_gt), (-2.5, ALU.is_lt),
                                 (2.5, ALU.is_gt)):
                    em.op("vector", lambda V, dq=dq, ea=ea, thr=thr, op0=op0:
                          V.scalar_tensor_tensor(
                              ea, dq, thr, ea, op0, ALU.subtract), s_v)
                em.op("vector", lambda V, SYp=SYp, ea=ea, nq=nq:
                      V.tensor_add(SYp, ea, nq), s_v)
                g_done[par] = s_v.n
            q2_all[i] = s_v.n

        def stage_D1(i, mid=None):
            ip = i % 2
            SY = sy2v[ip]
            em.wait("tensor", s_v, q2_all[i])
            for tt in range(2):
                em.wait("vector", s_s, col_read[tt])
                em.op("vector", lambda V, tt=tt, ip=ip: V.scalar_tensor_tensor(
                    col(32 + tt), amx[:, ip * 2 + tt:ip * 2 + tt + 1],
                    1.0 / 127.0, w2r_c[:, 0:1], ALU.mult, ALU.mult), s_v)
                rc_done = s_v.n
                for half in range(2):
                    bk = tt * 2 + half
                    pb = f2s[bk]
                    em.wait("tensor", s_s, f2_evac[bk])
                    for cc in range(NCH):
                        em.op("tensor", lambda T, pb=pb, cc=cc, tt=tt, SY=SY,
                              half=half: T.matmul(
                                  pb[:], SY[:, cc, tt * 128:(tt + 1) * 128],
                                  t2v[:, cc, half * 512:(half + 1) * 512],
                                  start=(cc == 0), stop=(cc == NCH - 1)), s_t)
                    f2_mm[bk] = s_t.n
                    em.wait("scalar", s_t, f2_mm[bk])
                    em.wait("scalar", s_v, rc_done)
                    em.wait("scalar", s_v, o2_vread[ip])
                    em.op("scalar", lambda Sc, pb=pb, tt=tt, ip=ip, half=half:
                          Sc.activation(
                              o2[:, ip * 2 * D + tt * D + half * 512:
                                 ip * 2 * D + tt * D + half * 512 + 512],
                              pb[:], AF.Identity, scale=col(32 + tt)), s_s)
                    f2_evac[bk] = s_s.n
                    col_read[tt] = s_s.n
                    if bk == 2 and mid is not None:
                        mid()
            ffn2_done[i] = s_t.n
            o2_rdy[i] = s_s.n

        def stage_D2(i):
            ip = i % 2
            bb = (i * TW) // S
            s0 = (i * TW) % S
            em.wait("sync", s_v, x1_read[0])
            em.dma("sync", x1F[:].rearrange("p (t d) -> p t d", t=2),
                   x1_h[bb, s0:s0 + TW, :].rearrange(
                       "(t p) d -> p t d", p=128), s_ld)
            x1_ld = s_ld.n
            o2p = o2[:, ip * 2 * D:(ip + 1) * 2 * D]
            o3 = o2p.rearrange("p (t d) -> p t d", t=2)
            cS = colp[:, 16:18]
            cQ = colp[:, 18:20]
            cMu = colp[:, 20:22]
            cM2 = colp[:, 22:24]
            cVar = colp[:, 24:26]
            cSd = colp[:, 26:28]
            cRs = colp[:, 28:30]
            cB = colp[:, 30:32]
            ln3 = lnsq[:].rearrange("p (t d) -> p t d", t=2)
            em.wait("scalar", s_s, o2_rdy[i])
            em.wait("scalar", s_v, aF_read[0])
            em.op("scalar", lambda Sc, o2p=o2p: Sc.activation(
                lnsq[:], o2p, AF.Square), s_s)
            sq_cnt = s_s.n
            em.wait("vector", s_s, o2_rdy[i])
            em.opw("vector", lambda V, cS=cS, o3=o3: V.reduce_sum(
                cS.unsqueeze(2), o3, AX.X), s_v)
            o2_vread[ip] = s_v.n
            em.wait("vector", s_s, sq_cnt)
            em.opw("vector", lambda V, cQ=cQ, ln3=ln3: V.reduce_sum(
                cQ.unsqueeze(2), ln3, AX.X), s_v)
            em.opw("vector", lambda V, cMu=cMu, cS=cS: V.tensor_scalar(
                cMu, cS, 1.0 / D, None, ALU.mult), s_v)
            em.opw("vector", lambda V, cM2=cM2, cMu=cMu: V.tensor_mul(
                cM2, cMu, cMu), s_v)
            em.opw("vector", lambda V, cVar=cVar, cQ=cQ, cM2=cM2:
                  V.scalar_tensor_tensor(
                      cVar, cQ, 1.0 / D, cM2, ALU.mult, ALU.subtract), s_v)
            em.opw("vector", lambda V, cVar=cVar: V.tensor_scalar(
                cVar, cVar, 1e-5, None, ALU.add), s_v)
            vc = s_v.n
            em.wait("scalar", s_v, vc)
            em.op("scalar", lambda Sc, cSd=cSd, cVar=cVar: Sc.activation(
                cSd, cVar, AF.Sqrt), s_s)
            sd_cnt = s_s.n
            em.wait("vector", s_s, sd_cnt)
            em.opw("vector", lambda V, cRs=cRs, cSd=cSd: V.reciprocal(
                cRs, cSd), s_v)
            em.opw("vector", lambda V, cB=cB, cMu=cMu, cRs=cRs:
                  V.scalar_tensor_tensor(
                      cB, cMu, -1.0, cRs, ALU.mult, ALU.mult), s_v)
            vc = s_v.n
            em.wait("scalar", s_v, vc)
            for tt in range(2):
                em.op("scalar", lambda Sc, tt=tt, o3=o3, cB=cB, cRs=cRs:
                      Sc.activation(
                          aF[:, tt * D:tt * D + D], o3[:, tt, :], AF.Identity,
                          bias=cB[:, tt:tt + 1], scale=cRs[:, tt:tt + 1]), s_s)
            ln_cnt = s_s.n
            em.wait("vector", s_s, ln_cnt)
            em.wait("vector", s_ld, x1_ld)
            em.wait("vector", s_st, xo_dma[0])
            em.op("vector", lambda V: V.tensor_add(xoF[:], aF[:], x1F[:]), s_v)
            aF_read[0] = s_v.n
            x1_read[0] = s_v.n
            addc = s_v.n
            em.wait("gpsimd", s_v, addc)
            em.dma("gpsimd", out_d[bb, s0:s0 + TW, :].rearrange(
                "(t p) d -> p t d", p=128),
                   xoF[:].rearrange("p (t d) -> p t d", t=2), s_st)
            xo_dma[0] = s_st.n

        emit_xw_load(0)
        build_rs1b(0)
        stage_A1(0)
        stage_A2(0)
        stage_B(0)
        for i in range(1, NTF):
            stage_A1(i)
            stage_C(i - 1)
            stage_A2(i)
            if i >= 2:
                stage_D2(i - 2)
            stage_D1(i - 1, mid=lambda i=i: stage_B(i))
        stage_C(NTF - 1)
        stage_D1(NTF - 1)
        stage_D2(NTF - 2)
        stage_D2(NTF - 1)

    es_f1.close()
    es_ff.close()
    em.wait("gpsimd", s_st)

    with nc.Block() as block:
        block.sync(lambda EE: [f(EE) for f in em.q["sync"]])
        block.tensor(lambda EE: [f(EE) for f in em.q["tensor"]])
        block.vector(lambda EE: [f(EE) for f in em.q["vector"]])
        block.scalar(lambda EE: [f(EE) for f in em.q["scalar"]])
        block.gpsimd(lambda EE: [f(EE) for f in em.q["gpsimd"]])
    es.close()
    return nc


def kernel(L0_attn_x, L0_attn_wq, L0_attn_bq, L0_attn_wk, L0_attn_bk,
           L0_attn_wv, L0_attn_bv, L0_attn_wo, L0_attn_bo, L0_ffn_w1,
           L0_ffn_conv_w, L0_ffn_conv_b, L0_ffn_w2):
    x = np.ascontiguousarray(np.asarray(L0_attn_x, dtype=np.float32))
    w = {k: np.ascontiguousarray(np.asarray(v, dtype=np.float32)) for k, v in [
        ("wq", L0_attn_wq), ("bq", L0_attn_bq), ("wk", L0_attn_wk),
        ("bk", L0_attn_bk), ("wv", L0_attn_wv), ("bv", L0_attn_bv),
        ("wo", L0_attn_wo), ("bo", L0_attn_bo), ("w1", L0_ffn_w1),
        ("cw", L0_ffn_conv_w), ("cb", L0_ffn_conv_b), ("w2", L0_ffn_w2)]}
    nc = bass.Bass("TRN2", target_bir_lowering=False,
                   detect_race_conditions=False)
    build(nc)
    in_maps = [dict(x=x[2 * i:2 * i + 2], **w) for i in range(8)]
    res = run_bass_kernel_spmd(nc, in_maps, core_ids=list(range(8)))
    return np.concatenate([r["out"] for r in res.results], axis=0)


# version 15
# speedup vs baseline: 1.1963x; 1.1963x over previous
import numpy as np
import ml_dtypes
import concourse.bass as bass
import concourse.mybir as mybir
from concourse.bass_utils import run_bass_kernel_spmd

F32 = mybir.dt.float32
BF16 = mybir.dt.bfloat16
I32 = mybir.dt.int32
FP8 = mybir.dt.float8e4
AF = mybir.ActivationFunctionType
ALU = mybir.AluOpType
AX = mybir.AxisListType

B = 2
S = 2048
D = 1024
H = 16
E = 64
DH = 4096
NT = S // 128
NG = S // 512
NC8 = D // 128
NCH = DH // 128
VS = 68
PVS = 72
CSHIFT = 24.0

ENGS = ("sync", "tensor", "vector", "scalar", "gpsimd")
DBG = False


class Sem:
    def __init__(self, h):
        self.h = h
        self.n = 0


class Em:

    def __init__(self):
        self.q = {e: [] for e in ENGS}
        self.waited = {e: {} for e in ENGS}

    def emit(self, eng, fn):
        self.q[eng].append(fn)

    def wait(self, eng, sem, val=None):
        val = sem.n if val is None else val
        if val <= 0:
            return
        key = id(sem)
        if self.waited[eng].get(key, -1) >= val:
            return
        self.waited[eng][key] = val
        h = sem.h
        self.emit(eng, lambda EE, h=h, val=val: EE.wait_ge(h, val))

    def op(self, eng, fn, inc=None, incn=1):
        if inc is None:
            self.emit(eng, fn)
            return None
        inc.n += incn
        h = inc.h

        def g(EE, fn=fn, h=h, incn=incn):
            fn(EE).then_inc(h, incn)

        self.emit(eng, g)
        return inc.n

    def opw(self, eng, fn, inc):
        n = self.op(eng, fn, inc)
        self.waited[eng][id(inc)] = n
        h = inc.h
        self.emit(eng, lambda EE, h=h, n=n: EE.wait_ge(h, n))
        return n

    def dma(self, eng, out, in_, inc, slow=False):
        return self.op(eng, lambda EE, out=out, in_=in_, slow=slow: EE.dma_start(
            out=out, in_=in_, allow_slow_non_contiguous=slow), inc, 16)

    def barrier(self, sems):
        for e in ENGS:
            for s in sems:
                self.wait(e, s)


def build(nc):
    bf = ml_dtypes.bfloat16
    x_d = nc.dram_tensor("x", [B, S, D], F32, kind="ExternalInput").ap()
    wq_d = nc.dram_tensor("wq", [H, D, E], F32, kind="ExternalInput").ap()
    bq_d = nc.dram_tensor("bq", [H, 1, E], F32, kind="ExternalInput").ap()
    wk_d = nc.dram_tensor("wk", [H, D, E], F32, kind="ExternalInput").ap()
    bk_d = nc.dram_tensor("bk", [H, 1, E], F32, kind="ExternalInput").ap()
    wv_d = nc.dram_tensor("wv", [H, D, E], F32, kind="ExternalInput").ap()
    bv_d = nc.dram_tensor("bv", [H, 1, E], F32, kind="ExternalInput").ap()
    wo_d = nc.dram_tensor("wo", [D, D], F32, kind="ExternalInput").ap()
    bo_d = nc.dram_tensor("bo", [D], F32, kind="ExternalInput").ap()
    w1_d = nc.dram_tensor("w1", [DH, D], F32, kind="ExternalInput").ap()
    cw_d = nc.dram_tensor("cw", [DH, 1, 3], F32, kind="ExternalInput").ap()
    cb_d = nc.dram_tensor("cb", [DH], F32, kind="ExternalInput").ap()
    w2_d = nc.dram_tensor("w2", [D, DH], F32, kind="ExternalInput").ap()
    out_d = nc.dram_tensor("out", [B, S, D], F32, kind="ExternalOutput").ap()

    x1_h = nc.dram_tensor("x1h", [B, S, D], F32, kind="Internal").ap()
    xq_h = nc.dram_tensor("xqh", [NC8, 128, B * S], BF16, kind="Internal").ap()
    t1_h = nc.dram_tensor("t1h", [NC8, 128, DH], FP8, kind="Internal").ap()
    t2_h = nc.dram_tensor("t2h", [NCH, 128, D], FP8, kind="Internal").ap()
    wo_h = nc.dram_tensor("woh", [128, NC8 * D], BF16, kind="Internal").ap()

    ident_b = nc.inline_tensor(np.eye(128, dtype=np.float32).astype(bf),
                               name="identb").ap()
    onesrow_b = nc.inline_tensor(np.ones((1, 128), np.float32).astype(bf),
                                 name="onesrb").ap()
    onesrow_f = nc.inline_tensor(np.ones((1, 128), np.float32), name="onesrf").ap()
    onescol_b = nc.inline_tensor(np.ones((128, 1), np.float32).astype(bf),
                                 name="onescb").ap()
    onescol_f = nc.inline_tensor(np.ones((128, 1), np.float32), name="onescf").ap()
    ident_f = nc.inline_tensor(np.eye(128, dtype=np.float32), name="identf").ap()

    em = Em()
    from contextlib import ExitStack

    def tap(name, ap, shape, dt):
        if not DBG or (isinstance(DBG, (set, list)) and name not in DBG):
            return
        dd = nc.dram_tensor("dbg_" + name, list(shape), dt, kind="ExternalOutput").ap()
        for s in ALLSEMS[2:]:
            em.wait("gpsimd", s)
        em.dma("gpsimd", dd, ap, s_st)

    es = ExitStack()

    def sb(name, shape, dt):
        return es.enter_context(nc.sbuf_tensor(name, list(shape), dt))

    s_ld = Sem(es.enter_context(nc.semaphore("s_ld")))
    s_st = Sem(es.enter_context(nc.semaphore("s_st")))
    s_t = Sem(es.enter_context(nc.semaphore("s_t")))
    s_v = Sem(es.enter_context(nc.semaphore("s_v")))
    s_s = Sem(es.enter_context(nc.semaphore("s_s")))
    s_g = Sem(es.enter_context(nc.semaphore("s_g")))
    ALLSEMS = (s_ld, s_st, s_t, s_v, s_s, s_g)

    identb = sb("identb_s", (128, 128), BF16)
    identf = sb("identf_s", (128, 128), F32)
    onesrb = sb("onesrb_s", (1, 128), BF16)
    onesrf = sb("onesrf_s", (1, 128), F32)
    onescb = sb("onescb_s", (128, 1), BF16)
    onescf = sb("onescf_s", (128, 1), F32)
    bq_c = sb("bq_c", (128, H // 2), F32)
    bk_c = sb("bk_c", (128, H // 2), F32)
    cw_c = sb("cw_c", (128, NCH * 3), F32)
    cwb_c = sb("cwb_c", (128, NCH * 3), BF16)
    cb_c = sb("cb_c", (128, NCH), F32)
    w1r_c = sb("w1r_c", (128, 1), F32)
    w1s_c = sb("w1s_c", (128, 1), F32)
    w2r_c = sb("w2r_c", (128, 1), F32)
    w2s_c = sb("w2s_c", (128, 1), F32)
    negC = sb("negC", (128, H), F32)
    mxq = sb("mxq", (1, H), F32)
    mxk = sb("mxk", (1, H), F32)
    t11a = sb("t11a", (1, 4), F32)
    t11b = sb("t11b", (1, 4), F32)
    colp = sb("colp", (128, 40), F32)

    def col(i):
        return colp[:, i:i + 1]

    def ln_chain(par, src, dst, extra_wait=None):
        c0, c1, c2, c3, c4, c5, c6, c7 = (col(par * 8 + k) for k in range(8))
        em.opw("vector", lambda V: V.reduce_sum(c0, src, AX.X), s_v)
        em.wait("scalar", s_v)
        if extra_wait is not None:
            extra_wait()
        em.op("scalar", lambda Sc: Sc.activation(lnsq[:, 0:D], src, AF.Square), s_s)
        em.wait("vector", s_s)
        em.opw("vector", lambda V: V.reduce_sum(c1, lnsq[:, 0:D], AX.X), s_v)
        em.opw("vector", lambda V: V.tensor_scalar(c2, c0, 1.0 / D, None, ALU.mult), s_v)
        em.opw("vector", lambda V: V.tensor_mul(c3, c2, c2), s_v)
        em.opw("vector", lambda V: V.scalar_tensor_tensor(
            c4, c1, 1.0 / D, c3, ALU.mult, ALU.subtract), s_v)
        em.opw("vector", lambda V: V.tensor_scalar(c4, c4, 1e-5, None, ALU.add), s_v)
        em.wait("scalar", s_v)
        em.op("scalar", lambda Sc: Sc.activation(c5, c4, AF.Sqrt), s_s)
        em.wait("vector", s_s)
        em.opw("vector", lambda V: V.reciprocal(c6, c5), s_v)
        em.opw("vector", lambda V: V.scalar_tensor_tensor(
            c7, c2, -1.0, c6, ALU.mult, ALU.mult), s_v)
        em.wait("scalar", s_v)
        em.op("scalar", lambda Sc: Sc.activation(
            dst, src, AF.Identity, bias=c7, scale=c6), s_s)
        return s_s.n

    em.dma("sync", identb[:], ident_b, s_ld)
    em.dma("sync", identf[:], ident_f, s_ld)
    em.dma("sync", onesrb[:], onesrow_b, s_ld)
    em.dma("sync", onesrf[:], onesrow_f, s_ld)
    em.dma("sync", onescb[:], onescol_b, s_ld)
    em.dma("sync", onescf[:], onescol_f, s_ld)
    em.dma("sync", bq_c[:], bq_d.flatten().rearrange("(pr p) -> p pr", p=128), s_ld, slow=True)
    em.dma("sync", bk_c[:], bk_d.flatten().rearrange("(pr p) -> p pr", p=128), s_ld, slow=True)
    em.dma("sync", cw_c[:].rearrange("p (cc k) -> p cc k", k=3), cw_d.rearrange("(cc p) o k -> p cc (o k)", p=128), s_ld, slow=True)
    em.dma("sync", cb_c[:], cb_d.rearrange("(cc p) -> p cc", p=128), s_ld, slow=True)
    pslow_ld = s_ld.n
    em.wait("vector", s_ld, pslow_ld - 4 * 16)

    es_w = ExitStack()
    stg = es_w.enter_context(nc.sbuf_tensor("stg", [128, NC8 * D], F32))
    wo_nat = es_w.enter_context(nc.sbuf_tensor("wo_nat", [128, NC8 * D], BF16))
    wt = es_w.enter_context(nc.sbuf_tensor("wt", [128, DH], F32))
    qt = es_w.enter_context(nc.sbuf_tensor("qt", [128, DH], F32))
    qn = es_w.enter_context(nc.sbuf_tensor("qn", [128, DH], I32))
    qd = es_w.enter_context(nc.sbuf_tensor("qd", [128, DH], F32))
    qp = es_w.enter_context(nc.sbuf_tensor("qp", [128, DH], F32))
    qm = es_w.enter_context(nc.sbuf_tensor("qm", [128, DH], BF16))
    tb = es_w.enter_context(nc.sbuf_tensor("tb", [128, DH], BF16))
    ts8 = es_w.enter_context(nc.sbuf_tensor("ts8", [128, 1024], FP8))
    tsb = es_w.enter_context(nc.sbuf_tensor("tsb", [128, 1024], FP8))
    ptr_w = es_w.enter_context(nc.psum_tensor("ptr_w", [128, 1024], BF16))
    pone = es_w.enter_context(nc.psum_tensor("pone", [128, 512], F32))
    if True:
        Wo_T = es_w.enter_context(nc.sbuf_tensor("Wo_T", [128, NC8 * D], BF16))
        em.dma("scalar", stg[:].rearrange("p (oc i) -> p oc i", oc=NC8),
           wo_d.rearrange("(oc p) i -> p oc i", p=128), s_ld)
        em.wait("vector", s_ld)
        em.op("vector", lambda V: V.tensor_copy(wo_nat[:], stg[:]), s_v)
        em.wait("tensor", s_v)
        for oc in range(NC8):
            for ic in range(NC8):
                em.op("tensor", lambda T, oc=oc, ic=ic: T.transpose(
                    ptr_w[:, ic * 128:(ic + 1) * 128],
                    wo_nat[:, oc * D + ic * 128:oc * D + (ic + 1) * 128],
                    identb[:]), s_t)
            tcnt = s_t.n
            em.wait("vector", s_t, tcnt)
            em.op("vector", lambda V, oc=oc: V.tensor_copy(
                Wo_T[:].rearrange("p (ic o) -> p ic o", ic=NC8)[
                    :, :, oc * 128:(oc + 1) * 128],
                ptr_w[:].rearrange("p (c t) -> p c t", c=NC8)), s_v)
            em.wait("tensor", s_v)

        em.wait("gpsimd", s_v)
        em.dma("gpsimd", wo_h, Wo_T[:], s_st)

        def wprep(w_ap, nrow, scol, rcol, thbm, tstage, nelem):
            ncol = w_ap.shape[1]
            ntile = nrow // 128
            em.op("vector", lambda V: V.memset(col(0), 0.0), s_v)
            for it in range(ntile):
                buf = wt[:, 0:ncol]
                em.wait("scalar", s_v)
                em.dma("scalar", buf, w_ap[it * 128:(it + 1) * 128, :], s_ld)
                em.wait("vector", s_ld)
                em.opw("vector", lambda V, buf=buf: V.reduce_sum(
                    col(1), buf, AX.X, apply_absolute_value=True), s_v)
                em.opw("vector", lambda V: V.tensor_add(col(0), col(0), col(1)), s_v)
            em.wait("tensor", s_v)
            em.op("tensor", lambda T: T.matmul(
                pone[:1, 0:1], colp[:, 0:1], onescf[:, 0:1],
                start=True, stop=True), s_t)
            em.wait("vector", s_t)
            em.opw("vector", lambda V: V.tensor_scalar(
                t11a[:, 0:1], pone[:1, 0:1], 1.0 / nelem, 1e-5, ALU.mult, ALU.max), s_v)
            em.opw("vector", lambda V: V.reciprocal(t11a[:, 1:2], t11a[:, 0:1]), s_v)
            em.wait("tensor", s_v)
            em.op("tensor", lambda T: T.matmul(
                pone[:, 0:1], onesrf[:1, :], t11a[:1, 0:1], start=True, stop=True), s_t)
            em.op("tensor", lambda T: T.matmul(
                pone[:, 1:2], onesrf[:1, :], t11a[:1, 1:2], start=True, stop=True), s_t)
            em.wait("vector", s_t)
            em.op("vector", lambda V: V.tensor_copy(rcol, pone[:, 0:1]), s_v)
            em.op("vector", lambda V: V.tensor_copy(scol, pone[:, 1:2]), s_v)
            em.wait("tensor", s_v)
            for it in range(ntile):
                buf = wt[:, 0:ncol]
                em.wait("scalar", s_v)
                em.dma("scalar", buf, w_ap[it * 128:(it + 1) * 128, :], s_ld)
                em.wait("vector", s_ld)
                T_ = qt[:, :ncol]; N_ = qn[:, :ncol]; Dd = qd[:, :ncol]
                P_ = qp[:, :ncol]; M_ = qm[:, :ncol]; TB = tb[:, :ncol]
                em.wait("vector", s_t)
                em.op("vector", lambda V, buf=buf, T_=T_: V.tensor_scalar(
                    T_, buf, scol, None, ALU.mult), s_v)
                em.op("vector", lambda V, T_=T_, N_=N_: V.tensor_copy(N_, T_), s_v)
                em.op("vector", lambda V, T_=T_, N_=N_, Dd=Dd: V.tensor_sub(Dd, T_, N_), s_v)
                em.op("vector", lambda V, Dd=Dd, N_=N_, P_=P_: V.scalar_tensor_tensor(
                    P_, Dd, 0.5, N_, ALU.is_gt, ALU.add), s_v)
                em.op("vector", lambda V, Dd=Dd, M_=M_: V.tensor_scalar(
                    M_, Dd, -0.5, None, ALU.is_lt), s_v)
                em.op("vector", lambda V, P_=P_, M_=M_: V.tensor_sub(P_, P_, M_), s_v)
                em.op("vector", lambda V, P_=P_, TB=TB: V.tensor_scalar(
                    TB, P_, 1.0, -1.0, ALU.min, ALU.max), s_v)
                vq = s_v.n
                em.wait("tensor", s_v, vq)
                ntr = ncol // 128
                for j0 in range(0, ntr, 8):
                    jn = min(8, ntr - j0)
                    em.wait("tensor", s_v)
                    for j in range(jn):
                        em.op("tensor", lambda T, j0=j0, j=j, TB=TB: T.transpose(
                            ptr_w[:, j * 128:(j + 1) * 128],
                            TB[:, (j0 + j) * 128:(j0 + j + 1) * 128], identb[:]), s_t)
                    tcnt = s_t.n
                    em.wait("vector", s_t, tcnt)
                    em.wait("vector", s_st)
                    em.op("vector", lambda V, jn=jn: V.tensor_copy(
                        tstage[:, 0:jn * 128], ptr_w[:, 0:jn * 128]), s_v)
                    vcnt = s_v.n
                    em.wait("gpsimd", s_v, vcnt)
                    em.dma("gpsimd",
                           thbm[j0:j0 + jn, :, it * 128:(it + 1) * 128]
                           .rearrange("j p x -> p j x"),
                           tstage[:, 0:jn * 128].rearrange(
                               "p (j x) -> p j x", j=jn), s_st)

        wprep(w1_d, DH, w1s_c[:, 0:1], w1r_c[:, 0:1], t1_h, ts8, DH * D)
        wprep(w2_d, D, w2s_c[:, 0:1], w2r_c[:, 0:1], t2_h, tsb, DH * D)
        em.barrier(ALLSEMS)
    es_w.close()

    lnsq = sb("lnsq", (128, 2 * D), F32)
    es_a = ExitStack()

    def sba(name, shape, dt):
        return es_a.enter_context(nc.sbuf_tensor(name, list(shape), dt))

    bvrow = sba("bvrow", (1, D), BF16)
    borow = sba("borow", (1, D), BF16)
    Qp = [sba(f"Qp{i}", (128, S), BF16) for i in range(H // 2)]
    Kp = [sba(f"Kp{i}", (128, S), BF16) for i in range(H // 2)]
    Vt = sba("Vt", (128, NT * H * VS), BF16)
    Vt3 = Vt[:].rearrange("p (ct x) -> p ct x", ct=NT)
    em.dma("sync", lnsq[0:1, 0:D], bv_d.flatten()[None, :], s_ld)
    em.wait("vector", s_ld)
    em.op("vector", lambda V: V.tensor_copy(bvrow[:], lnsq[0:1, 0:D]), s_v)
    em.wait("sync", s_v)
    em.dma("sync", lnsq[0:1, 0:D], bo_d[None, :], s_ld)
    em.wait("vector", s_ld)
    em.op("vector", lambda V: V.tensor_copy(borow[:], lnsq[0:1, 0:D]), s_v)
    em.op("vector", lambda V: V.memset(Vt[:], 0.0), s_v)
    em.op("vector", lambda V: V.memset(
        Vt[:].rearrange("p (cth e) -> p cth e", e=VS)[:, :, 64:65], 1.0), s_v)

    def _batch(b):
        es_pw = ExitStack()
        wstg = es_pw.enter_context(nc.sbuf_tensor(f"wstg{b}", [128, NC8 * D // 2], F32))
        wq_s = es_pw.enter_context(nc.sbuf_tensor(f"wq_s{b}", [128, NC8 * D], BF16))
        wk_s = es_pw.enter_context(nc.sbuf_tensor(f"wk_s{b}", [128, NC8 * D], BF16))
        wv_s = es_pw.enter_context(nc.sbuf_tensor(f"wv_s{b}", [128, NC8 * D], BF16))
        if True:
            for wd, wsb, pairs in ((wq_d, wq_s, True), (wk_d, wk_s, True),
                                   (wv_d, wv_s, False)):
                for chalf in range(2):
                    em.wait("sync", s_v)
                    for h in range(H):
                        if pairs:
                            dst = wstg[:].rearrange(
                                "p (c pr he) -> p c pr he", c=NC8 // 2, pr=H // 2)[
                                :, :, h // 2, (h % 2) * 64:(h % 2) * 64 + 64]
                        else:
                            dst = wstg[:].rearrange(
                                "p (c he) -> p c he", c=NC8 // 2)[
                                :, :, h * 64:h * 64 + 64]
                        em.dma("sync", dst,
                               wd[h].rearrange("(c p) e -> p c e", p=128)[
                                   :, chalf * 4:chalf * 4 + 4, :], s_ld)
                    em.wait("vector", s_ld)
                    em.op("vector", lambda V, wsb=wsb, chalf=chalf: V.tensor_copy(
                        wsb[:, chalf * 4 * D:(chalf * 4 + 4) * D], wstg[:]), s_v)

            es_px = ExitStack()
            x_t = es_px.enter_context(nc.sbuf_tensor(f"x_t{b}", [128, 2 * D], F32))
            u_sb = es_px.enter_context(nc.sbuf_tensor(f"u_sb{b}", [128, D], BF16))
            u_fm = es_px.enter_context(nc.sbuf_tensor(f"u_fm{b}", [128, NC8 * 512], BF16))
            qsq = es_px.enter_context(nc.sbuf_tensor(f"qsq{b}", [128, S], BF16))
            ptr_a = es_px.enter_context(nc.psum_tensor(f"ptr_a{b}", [128, 1024], BF16))
            pq0 = es_px.enter_context(nc.psum_tensor(f"pq0{b}", [128, 512]))
            pq1 = es_px.enter_context(nc.psum_tensor(f"pq1{b}", [128, 512]))
            pk0 = es_px.enter_context(nc.psum_tensor(f"pk0{b}", [128, 512]))
            pk1 = es_px.enter_context(nc.psum_tensor(f"pk1{b}", [128, 512]))
            pv0 = es_px.enter_context(nc.psum_tensor(f"pv0{b}", [128, 512]))
            pv1 = es_px.enter_context(nc.psum_tensor(f"pv1{b}", [128, 512]))
            pn = es_px.enter_context(nc.psum_tensor(f"pn{b}", [128, 512]))
            if True:
                ld_q = []

                def xload(b, k):
                    em.wait("sync", s_s)
                    em.dma("sync", x_t[:, (k % 2) * D:(k % 2) * D + D],
                           x_d[b, k * 128:(k + 1) * 128, :], s_ld)
                    ld_q.append(s_ld.n)

                xload(b, 0)
                xload(b, 1)
                for g in range(NG):
                    ufm3 = u_fm[:].rearrange("p (c t) -> p c t", c=NC8)
                    for tt in range(4):
                        kk = g * 4 + tt
                        xt = x_t[:, (kk % 2) * D:(kk % 2) * D + D]
                        ut = u_sb[:, 0:D]
                        em.wait("vector", s_ld, ld_q[kk])
                        em.wait("scalar", s_ld, ld_q[kk])
                        em.wait("scalar", s_t)
                        ln_chain(tt % 2, xt, ut)
                        if b == 0 and g == 0 and tt == 0:
                            tap("xt0", xt, (128, D), F32)
                            tap("colp0", colp[:], (128, 40), F32)
                            tap("usb0", ut, (128, D), BF16)
                        scnt = s_s.n
                        em.wait("tensor", s_s, scnt)
                        for ic in range(NC8):
                            em.op("tensor", lambda T, ic=ic, ut=ut: T.transpose(
                                ptr_a[:, ic * 128:(ic + 1) * 128],
                                ut[:, ic * 128:(ic + 1) * 128], identb[:]), s_t)
                        tcnt = s_t.n
                        em.wait("vector", s_t, tcnt)
                        em.op("vector", lambda V, ufm3=ufm3, tt=tt: V.tensor_copy(
                            ufm3[:, :, tt * 128:(tt + 1) * 128],
                            ptr_a[:].rearrange("p (c t) -> p c t", c=NC8)), s_v)
                        em.wait("tensor", s_v)
                        if kk + 2 < NT:
                            xload(b, kk + 2)
                    em.wait("tensor", s_v)
                    em.wait("scalar", s_ld, pslow_ld)
                    for wsb, pbank, dstl, bcol in (
                            (wq_s, (pq0, pq1), Qp, bq_c),
                            (wk_s, (pk0, pk1), Kp, bk_c)):
                        wsb4 = wsb[:].rearrange(
                            "p (c pr x) -> p c pr x", c=NC8, pr=H // 2)
                        for pr in range(H // 2):
                            pb = pbank[pr % 2]
                            em.wait("tensor", s_s)
                            for c in range(NC8):
                                em.op("tensor", lambda T, pb=pb, wsb4=wsb4, c=c,
                                      pr=pr, ufm3=ufm3: T.matmul(
                                          pb[:], wsb4[:, c, pr, :], ufm3[:, c, :],
                                          start=(c == 0), stop=(c == NC8 - 1)), s_t)
                            tcnt = s_t.n
                            em.wait("scalar", s_t, tcnt)
                            em.op("scalar", lambda Sc, pb=pb, dstl=dstl, pr=pr, g=g,
                                  bcol=bcol: Sc.activation(
                                      dstl[pr][:, g * 512:(g + 1) * 512], pb[:],
                                      AF.Identity, bias=bcol[:, pr:pr + 1], scale=1.0), s_s)
                    wv3 = wv_s[:].rearrange("p (c x) -> p c x", c=NC8)
                    for tt in range(4):
                        ct = g * 4 + tt
                        for half in range(2):
                            pb = (pv0, pv1)[half]
                            em.wait("tensor", s_v)
                            em.op("tensor", lambda T, pb=pb, half=half: T.matmul(
                                pb[:], onesrb[:1, :],
                                bvrow[:1, half * 512:(half + 1) * 512],
                                start=True, stop=False), s_t)
                            for c in range(NC8):
                                em.op("tensor", lambda T, pb=pb, wv3=wv3, c=c,
                                      half=half, ufm3=ufm3, tt=tt: T.matmul(
                                          pb[:], ufm3[:, c, tt * 128:(tt + 1) * 128],
                                          wv3[:, c, half * 512:(half + 1) * 512],
                                          start=False, stop=(c == NC8 - 1)), s_t)
                            tcnt = s_t.n
                            em.wait("vector", s_t, tcnt)
                            em.op("vector", lambda V, pb=pb, ct=ct, half=half:
                                  V.tensor_copy(
                                      Vt3[:, ct, :].rearrange(
                                          "p (h e) -> p h e", e=VS)[
                                          :, half * 8:(half + 1) * 8, 0:64],
                                      pb[:].rearrange("p (h e) -> p h e", e=64)), s_v)

                em.op("vector", lambda V: V.memset(mxq[:], 0.0), s_v)
                em.op("vector", lambda V: V.memset(mxk[:], 0.0), s_v)
                for src, mx in ((Qp, mxq), (Kp, mxk)):
                    for pr in range(H // 2):
                        em.wait("scalar", s_v)
                        em.wait("scalar", s_t)
                        em.op("scalar", lambda Sc, src=src, pr=pr: Sc.activation(
                            qsq[:], src[pr][:], AF.Square), s_s)
                        scnt = s_s.n
                        em.wait("tensor", s_s, scnt)
                        for h2 in range(2):
                            h = pr * 2 + h2
                            for sl in range(4):
                                em.wait("tensor", s_v)
                                em.op("tensor", lambda T, h2=h2, sl=sl: T.matmul(
                                    pn[:1, :], onescb[h2 * 64:h2 * 64 + 64, 0:1],
                                    qsq[h2 * 64:h2 * 64 + 64, sl * 512:(sl + 1) * 512],
                                    start=True, stop=True,
                                    tile_position=(h2 * 64, 0)), s_t)
                                tcnt = s_t.n
                                em.wait("vector", s_t, tcnt)
                                em.opw("vector", lambda V: V.reduce_max(
                                    t11a[:, 0:1], pn[:1, :], AX.X), s_v)
                                em.opw("vector", lambda V, mx=mx, h=h: V.tensor_max(
                                    mx[:, h:h + 1], mx[:, h:h + 1], t11a[:, 0:1]), s_v)
                if b == 0:
                    tap("q0", Qp[0][:], (128, S), BF16)
                    tap("k0", Kp[0][:], (128, S), BF16)
                    tap("vt", Vt[:], (128, NT * H * VS), BF16)
                    tap("ufm", u_fm[:], (128, NC8 * 512), BF16)
                    tap("mxq", mxq[:], (1, H), F32)
                    tap("mxk", mxk[:], (1, H), F32)
                for h in range(H):
                    em.opw("vector", lambda V, h=h: V.tensor_mul(
                        t11a[:, 0:1], mxq[:, h:h + 1], mxk[:, h:h + 1]), s_v)
                    em.wait("scalar", s_v)
                    em.op("scalar", lambda Sc: Sc.activation(
                        t11a[:, 1:2], t11a[:, 0:1], AF.Sqrt), s_s)
                    em.wait("vector", s_s)
                    em.opw("vector", lambda V: V.tensor_scalar(
                        t11b[:, 0:1], t11a[:, 1:2], -0.125, CSHIFT,
                        ALU.mult, ALU.add), s_v)
                    vcnt = s_v.n
                    em.wait("tensor", s_v, vcnt)
                    em.wait("tensor", s_v)
                    em.op("tensor", lambda T: T.matmul(
                        pn[:, 0:1], onesrf[:1, :], t11b[:1, 0:1],
                        start=True, stop=True), s_t)
                    tcnt = s_t.n
                    em.wait("vector", s_t, tcnt)
                    em.op("vector", lambda V, h=h: V.tensor_copy(
                        negC[:, h:h + 1], pn[:, 0:1]), s_v)
            if b == 0:
                tap("negc", negC[:], (128, H), F32)
            es_px.close()
        es_pw.close()
        em.barrier(ALLSEMS)

        es_sc = ExitStack()
        wo_sb = es_sc.enter_context(nc.sbuf_tensor(f"wo_sb{b}", [128, NC8 * D], BF16))
        Pt = es_sc.enter_context(nc.sbuf_tensor(f"Pt{b}", [128, 2 * NT * 512], BF16))
        att_tm = es_sc.enter_context(nc.sbuf_tensor(f"att_tm{b}", [128, 4 * D], BF16))
        att_fm = es_sc.enter_context(nc.sbuf_tensor(f"att_fm{b}", [128, NC8 * 512], BF16))
        y_sb = es_sc.enter_context(nc.sbuf_tensor(f"y_sb{b}", [128, D], F32))
        a_sb = es_sc.enter_context(nc.sbuf_tensor(f"a_sb{b}", [128, D], F32))
        x0_t = es_sc.enter_context(nc.sbuf_tensor(f"x0_t{b}", [128, 2 * D], F32))
        x1_t = es_sc.enter_context(nc.sbuf_tensor(f"x1_t{b}", [128, 2 * D], F32))
        rl_c = es_sc.enter_context(nc.sbuf_tensor(f"rl_c{b}", [128, 2], F32))
        sc0 = es_sc.enter_context(nc.psum_tensor(f"sc0{b}", [128, 512]))
        sc1 = es_sc.enter_context(nc.psum_tensor(f"sc1{b}", [128, 512]))
        sc2 = es_sc.enter_context(nc.psum_tensor(f"sc2{b}", [128, 512]))
        sc3 = es_sc.enter_context(nc.psum_tensor(f"sc3{b}", [128, 512]))
        pvA = es_sc.enter_context(nc.psum_tensor(f"pvA{b}", [128, 2 * PVS]))
        pvB = es_sc.enter_context(nc.psum_tensor(f"pvB{b}", [128, 2 * PVS]))
        ptr_s = es_sc.enter_context(nc.psum_tensor(f"ptr_s{b}", [128, 1024], BF16))
        if True:
            em.dma("sync", wo_sb[:], wo_h, s_ld)
            em.wait("tensor", s_ld)
            scb = (sc0, sc1, sc2, sc3)
            Pt3 = Pt[:].rearrange("p (h c s) -> p h c s", h=2, c=NT)
            ld0_q = []

            def x0load(k):
                em.wait("sync", s_v)
                em.dma("sync", x0_t[:, (k % 2) * D:(k % 2) * D + D],
                       x_d[b, k * 128:(k + 1) * 128, :], s_ld)
                ld0_q.append(s_ld.n)

            x0load(0)
            x0load(1)
            for g in range(NG):
                em.wait("tensor", s_s)
                for pr in range(H // 2):
                    bank_exp = [0, 0, 0, 0]
                    for c in range(NT):
                        for h2 in range(2):
                            j = 2 * h2 + (c % 2)
                            pb = scb[j]
                            em.wait("tensor", s_s, bank_exp[j])
                            em.op("tensor", lambda T, pb=pb, pr=pr, h2=h2, c=c, g=g:
                                  T.matmul(
                                      pb[:],
                                      Kp[pr][h2 * 64:h2 * 64 + 64, c * 128:(c + 1) * 128],
                                      Qp[pr][h2 * 64:h2 * 64 + 64, g * 512:(g + 1) * 512],
                                      start=True, stop=True,
                                      tile_position=(h2 * 64, 0)), s_t)
                            tcnt = s_t.n
                            em.wait("scalar", s_t, tcnt)
                            h = 2 * pr + h2
                            em.op("scalar", lambda Sc, pb=pb, h2=h2, c=c, h=h:
                                  Sc.activation(
                                      Pt3[:, h2, c, :], pb[:], AF.Exp,
                                      bias=negC[:, h:h + 1], scale=0.125), s_s)
                            bank_exp[j] = s_s.n
                    exp_done = s_s.n
                    if b == 0 and g == 0 and pr == 0:
                        tap("pt", Pt[:], (128, 2 * NT * 512), BF16)
                    em.wait("tensor", s_s, exp_done)
                    for h2 in range(2):
                        h = 2 * pr + h2
                        for sl in range(4):
                            pb = (pvA, pvB)[sl % 2]
                            off = (sl // 2) * PVS
                            for c in range(NT):
                                em.op("tensor", lambda T, pb=pb, h2=h2, c=c, sl=sl,
                                      h=h, off=off: T.matmul(
                                          pb[:, off:off + 65],
                                          Pt3[:, h2, c, sl * 128:(sl + 1) * 128],
                                          Vt3[:, c, h * VS:h * VS + 65],
                                          start=(c == 0), stop=(c == NT - 1)), s_t)
                            tcnt = s_t.n
                            em.wait("vector", s_t, tcnt)
                            em.op("vector", lambda V, pb=pb, off=off, sl=sl:
                                  V.reciprocal(rl_c[:, sl % 2:sl % 2 + 1],
                                               pb[:, off + 64:off + 65]), s_v)
                            vcnt = s_v.n
                            em.wait("scalar", s_v, vcnt)
                            em.op("scalar", lambda Sc, pb=pb, off=off, sl=sl, h=h:
                                  Sc.activation(
                                      att_tm[:, sl * D + h * 64:sl * D + h * 64 + 64],
                                      pb[:, off:off + 64], AF.Identity,
                                      scale=rl_c[:, sl % 2:sl % 2 + 1]), s_s)
                            if sl >= 1:
                                em.wait("tensor", s_s)
                    em.wait("tensor", s_s)
                if b == 0 and g == 0:
                    tap("atttm", att_tm[:], (128, 4 * D), BF16)
                for sl in range(4):
                    for ic in range(NC8):
                        em.op("tensor", lambda T, sl=sl, ic=ic: T.transpose(
                            ptr_s[:, ic * 128:(ic + 1) * 128],
                            att_tm[:, sl * D + ic * 128:sl * D + (ic + 1) * 128],
                            identb[:]), s_t)
                    tcnt = s_t.n
                    em.wait("vector", s_t, tcnt)
                    em.op("vector", lambda V, sl=sl: V.tensor_copy(
                        att_fm[:].rearrange("p (c t) -> p c t", c=NC8)[
                            :, :, sl * 128:(sl + 1) * 128],
                        ptr_s[:].rearrange("p (c t) -> p c t", c=NC8)), s_v)
                    vcnt = s_v.n
                    em.wait("tensor", s_v, vcnt)
                    for half in range(2):
                        pb = (sc0, sc1)[half]
                        em.wait("tensor", s_s)
                        em.op("tensor", lambda T, pb=pb, half=half: T.matmul(
                            pb[:], onesrb[:1, :],
                            borow[:1, half * 512:(half + 1) * 512],
                            start=True, stop=False), s_t)
                        for ic in range(NC8):
                            em.op("tensor", lambda T, pb=pb, ic=ic, half=half, sl=sl:
                                  T.matmul(
                                      pb[:],
                                      att_fm[:, ic * 512 + sl * 128:ic * 512 + (sl + 1) * 128],
                                      wo_sb[:, ic * D + half * 512:ic * D + (half + 1) * 512],
                                      start=False, stop=(ic == NC8 - 1)), s_t)
                        tcnt = s_t.n
                        em.wait("scalar", s_t, tcnt)
                        em.wait("scalar", s_v)
                        em.op("scalar", lambda Sc, pb=pb, half=half: Sc.activation(
                            y_sb[:, half * 512:(half + 1) * 512], pb[:], AF.Identity), s_s)
                    if b == 0 and g == 0 and sl == 0:
                        tap("ysb", y_sb[:], (128, D), F32)
                    em.wait("vector", s_s)
                    em.wait("scalar", s_v)
                    ln_chain(sl % 2, y_sb[:], a_sb[:])
                    em.wait("vector", s_s)
                    kk0 = g * 4 + sl
                    em.wait("vector", s_ld, ld0_q[kk0])
                    x0s = x0_t[:, (kk0 % 2) * D:(kk0 % 2) * D + D]
                    x1s = x1_t[:, (kk0 % 2) * D:(kk0 % 2) * D + D]
                    if sl >= 2:
                        em.wait("vector", s_st, s_st.n - 16)
                    em.op("vector", lambda V, x0s=x0s, x1s=x1s: V.tensor_add(
                        x1s, a_sb[:], x0s), s_v)
                    vcnt = s_v.n
                    em.wait("gpsimd", s_v, vcnt)
                    em.dma("gpsimd",
                           x1_h[b, (g * 4 + sl) * 128:(g * 4 + sl + 1) * 128, :],
                           x1s, s_st)
                    if kk0 + 2 < NT:
                        x0load(kk0 + 2)
        es_sc.close()
        em.barrier(ALLSEMS)

    for b in range(B):
        _batch(b)

    es_a.close()

    es_ff = ExitStack()
    rs1row = es_ff.enter_context(nc.sbuf_tensor("rs1row", [1, B * S], F32))
    import os as _os
    if _os.environ.get("KDBG"):
        print("sbuf before F0:", nc.sbuf_bytes_remaining)
    es_f0 = ExitStack()
    f0x = es_f0.enter_context(nc.sbuf_tensor("f0x", [128, 2 * D], F32))
    u2 = es_f0.enter_context(nc.sbuf_tensor("u2", [128, D], F32))
    qtF = es_f0.enter_context(nc.sbuf_tensor("qtF", [128, D], F32))
    qnF = es_f0.enter_context(nc.sbuf_tensor("qnF", [128, D], I32))
    qdF = es_f0.enter_context(nc.sbuf_tensor("qdF", [128, D], F32))
    qpF = es_f0.enter_context(nc.sbuf_tensor("qpF", [128, D], F32))
    qmF = es_f0.enter_context(nc.sbuf_tensor("qmF", [128, D], F32))
    xhat = es_f0.enter_context(nc.sbuf_tensor("xhat", [128, D], BF16))
    xst = es_f0.enter_context(nc.sbuf_tensor("xst", [128, 2 * D], BF16))
    ptr_f = es_f0.enter_context(nc.psum_tensor("ptr_f", [128, 1024], BF16))
    ptr_r = es_f0.enter_context(nc.psum_tensor("ptr_r", [128, 512], F32))
    if True:
        for t in range(B * NT):
            bb, ss = t // NT, (t % NT) * 128
            xb = f0x[:, (t % 2) * D:(t % 2) * D + D]
            em.wait("sync", s_v)
            em.dma("sync", xb, x1_h[bb, ss:ss + 128, :], s_ld)
            em.wait("vector", s_ld)
            em.wait("scalar", s_ld)
            em.wait("scalar", s_v)
            ln_chain(t % 2, xb, u2[:])
            em.wait("vector", s_s)
            c8, c9, c10, c11 = col(16 + (t % 2) * 4), col(17 + (t % 2) * 4), \
                col(18 + (t % 2) * 4), col(19 + (t % 2) * 4)
            em.opw("vector", lambda V, c8=c8: V.reduce_max(
                c8, u2[:], AX.X, apply_absolute_value=True), s_v)
            em.opw("vector", lambda V, c8=c8: V.tensor_scalar(
                c8, c8, 1e-5, None, ALU.max), s_v)
            em.opw("vector", lambda V, c8=c8, c9=c9: V.reciprocal(c9, c8), s_v)
            em.opw("vector", lambda V, c9=c9, c10=c10: V.tensor_scalar(
                c10, c9, 127.0, None, ALU.mult), s_v)
            em.opw("vector", lambda V, c8=c8, c11=c11: V.tensor_scalar(
                c11, c8, 1.0 / 127.0, None, ALU.mult), s_v)
            em.op("vector", lambda V, c10=c10: V.tensor_scalar(
                qtF[:], u2[:], c10, None, ALU.mult), s_v)
            em.op("vector", lambda V: V.tensor_copy(qnF[:], qtF[:]), s_v)
            em.op("vector", lambda V: V.tensor_sub(qdF[:], qtF[:], qnF[:]), s_v)
            em.op("vector", lambda V: V.scalar_tensor_tensor(
                qpF[:], qdF[:], 0.5, qnF[:], ALU.is_gt, ALU.add), s_v)
            em.op("vector", lambda V: V.scalar_tensor_tensor(
                qpF[:], qdF[:], 1.5, qpF[:], ALU.is_gt, ALU.add), s_v)
            em.op("vector", lambda V: V.tensor_scalar(
                qmF[:], qdF[:], -0.5, None, ALU.is_lt), s_v)
            em.op("vector", lambda V: V.scalar_tensor_tensor(
                qmF[:], qdF[:], -1.5, qmF[:], ALU.is_lt, ALU.add), s_v)
            em.op("vector", lambda V: V.tensor_sub(qpF[:], qpF[:], qmF[:]), s_v)
            em.op("vector", lambda V: V.tensor_copy(xhat[:], qpF[:]), s_v)
            vcnt = s_v.n
            em.wait("tensor", s_v, vcnt)
            em.op("tensor", lambda T, c11=c11: T.transpose(
                ptr_r[:1, 0:128], c11, identf[:]), s_t)
            tcnt = s_t.n
            em.wait("vector", s_t, tcnt)
            em.op("vector", lambda V, bb=bb, ss=ss: V.tensor_copy(
                rs1row[:1, bb * S + ss:bb * S + ss + 128], ptr_r[:1, 0:128]), s_v)
            em.wait("tensor", s_v)
            for ic in range(NC8):
                em.op("tensor", lambda T, ic=ic: T.transpose(
                    ptr_f[:, ic * 128:(ic + 1) * 128],
                    xhat[:, ic * 128:(ic + 1) * 128], identb[:]), s_t)
            tcnt = s_t.n
            em.wait("vector", s_t, tcnt)
            if t >= 2:
                em.wait("vector", s_st, s_st.n - 8 * 16)
            xs = xst[:, (t % 2) * D:(t % 2) * D + D]
            em.op("vector", lambda V, xs=xs: V.tensor_copy(xs, ptr_f[:]), s_v)
            vcnt = s_v.n
            em.wait("tensor", s_v)
            em.wait("gpsimd", s_v, vcnt)
            tok0 = bb * S + ss
            for ic in range(NC8):
                em.dma("gpsimd", xq_h[ic, :, tok0:tok0 + 128],
                       xs[:, ic * 128:(ic + 1) * 128], s_st)
        em.barrier(ALLSEMS)
    es_f0.close()

    if _os.environ.get("KDBG"):
        print("sbuf before F1:", nc.sbuf_bytes_remaining)
    es_f1 = ExitStack()
    TW = 256
    TWW = TW + 2
    NTF = (B * S) // TW
    t1s = es_f1.enter_context(nc.sbuf_tensor("t1s", [128, NC8 * DH], FP8))
    t2s = es_f1.enter_context(nc.sbuf_tensor("t2s", [128, NCH * D], FP8))
    xw = es_f1.enter_context(nc.sbuf_tensor("xw", [128, 2 * NC8 * TWW], BF16))
    sy2 = es_f1.enter_context(nc.sbuf_tensor("sy2", [128, 2 * NCH * TW], BF16))
    hsc = es_f1.enter_context(nc.sbuf_tensor("hsc", [128, NCH * TWW], BF16))
    rs1b = es_f1.enter_context(nc.sbuf_tensor("rs1b", [128, 2 * TWW], BF16))
    y0s = es_f1.enter_context(nc.sbuf_tensor("y0s", [128, 2 * TW], BF16))
    y1s = es_f1.enter_context(nc.sbuf_tensor("y1s", [128, 2 * TW], BF16))
    mxp = es_f1.enter_context(nc.sbuf_tensor("mxp", [128, 2 * TW], BF16))
    mxn = es_f1.enter_context(nc.sbuf_tensor("mxn", [128, 2 * TW], BF16))
    rmx = es_f1.enter_context(nc.sbuf_tensor("rmx", [128, 2 * TW], BF16))
    rmn = es_f1.enter_context(nc.sbuf_tensor("rmn", [128, 2 * TW], BF16))
    s2row = es_f1.enter_context(nc.sbuf_tensor("s2row", [1, TW], F32))
    s2b = es_f1.enter_context(nc.sbuf_tensor("s2b", [128, TW], F32))
    tqP = [es_f1.enter_context(nc.sbuf_tensor(f"tqP{k}", [128, 2 * TW], F32))
           for k in range(2)]
    nqP = [es_f1.enter_context(nc.sbuf_tensor(f"nqP{k}", [128, 2 * TW], I32))
           for k in range(2)]
    dqP = [es_f1.enter_context(nc.sbuf_tensor(f"dqP{k}", [128, 2 * TW], BF16))
           for k in range(2)]
    eaP = [es_f1.enter_context(nc.sbuf_tensor(f"eaP{k}", [128, 2 * TW], BF16))
           for k in range(2)]
    s2bw = es_f1.enter_context(nc.sbuf_tensor("s2bw", [128, 2 * TW], F32))
    amx = es_f1.enter_context(nc.sbuf_tensor("amx", [128, 4], F32))
    rc2 = es_f1.enter_context(nc.sbuf_tensor("rc2", [128, 4], F32))
    o2 = es_f1.enter_context(nc.sbuf_tensor("o2", [128, 4 * D], BF16))
    aF = es_f1.enter_context(nc.sbuf_tensor("aF", [128, 2 * D], BF16))
    x1F = es_f1.enter_context(nc.sbuf_tensor("x1F", [128, 2 * D], F32))
    xoF = es_f1.enter_context(nc.sbuf_tensor("xoF", [128, 2 * D], F32))
    fAs = [es_f1.enter_context(nc.psum_tensor(f"fA{k}", [128, TWW])) for k in range(2)]
    f2s = [es_f1.enter_context(nc.psum_tensor(f"f2{k}", [128, 512])) for k in range(4)]
    pbc = es_f1.enter_context(nc.psum_tensor("pbc", [128, TWW]))
    ptr_q = es_f1.enter_context(nc.psum_tensor("ptr_q", [128, TW], BF16))
    if True:
        t1v = t1s[:].rearrange("p (c x) -> p c x", c=NC8)
        t2v = t2s[:].rearrange("p (c x) -> p c x", c=NCH)
        xwv = [xw[:, j * NC8 * TWW:(j + 1) * NC8 * TWW].rearrange(
            "p (c w) -> p c w", c=NC8) for j in range(2)]
        sy2v = [sy2[:, j * NCH * TW:(j + 1) * NCH * TW].rearrange(
            "p (c x) -> p c x", c=NCH) for j in range(2)]

        em.wait("vector", s_ld, pslow_ld)
        em.wait("scalar", s_ld, pslow_ld)
        em.op("vector", lambda V: V.tensor_copy(cwb_c[:], cw_c[:]), s_v)
        for c in range(NC8):
            em.dma("sync", t1v[:, c, :], t1_h[c], s_ld)
        for c in range(NCH):
            em.dma("sync", t2v[:, c, :], t2_h[c], s_ld)
        em.wait("tensor", s_ld)

        ld_xw = [0] * NTF
        ffn1_done = [0] * NTF
        ffn2_done = [0] * NTF
        o2_rdy = [0] * NTF
        o2_vread = [0, 0]
        tp_done = [0] * NTF
        q2_all = [0] * NTF
        rs1_rdy = [0] * NTF
        ds_last = [0] * NTF
        minmax_done = [0] * NTF
        hsc_y1v = [0] * NCH
        hsc_desc = [0] * NCH
        fa_evac = [0, 0]
        f2_mm = [0, 0, 0, 0]
        f2_evac = [0, 0, 0, 0]
        y0_read = [0, 0]
        y1_read = [0, 0]
        g_done = [0, 0]
        col_read = [0, 0]
        aF_read = [0, 0]
        x1_read = [0, 0]
        xo_dma = [0, 0]
        ptrq_read = [0]
        pbc_free = [0]
        ln_prev = [0, 0]

        def _edges(j):
            s0j = (j * TW) % S
            lo = 1 if s0j == 0 else 0
            hi = TWW - 1 if s0j + TW == S else TWW
            return lo, hi

        def emit_xw_load(j):
            jp = j % 2
            Wj = xwv[jp]
            lo, hi = _edges(j)
            tok0 = j * TW
            if j >= 2:
                em.wait("sync", s_t, ffn1_done[j - 2])
            for c in range(NC8):
                em.dma("sync", Wj[:, c, lo:hi],
                       xq_h[c, :, tok0 - 1 + lo:tok0 - 1 + hi], s_ld)
            ld_xw[j] = s_ld.n

        def build_rs1b(j):
            jp = j % 2
            lo, hi = _edges(j)
            tok0 = j * TW
            sl = rs1b[:, jp * TWW:jp * TWW + TWW]
            if j >= 2:
                em.wait("vector", s_g, ds_last[j - 2])
            if lo == 1:
                em.op("vector", lambda V, sl=sl: V.memset(sl[:, 0:1], 1.0), s_v)
            if hi == TWW - 1:
                em.op("vector", lambda V, sl=sl: V.memset(
                    sl[:, TWW - 1:TWW], 1.0), s_v)
            em.wait("tensor", s_v, pbc_free[0])
            em.op("tensor", lambda T, tok0=tok0, lo=lo, hi=hi: T.matmul(
                pbc[:, 0:hi - lo], onesrf[:1, :],
                rs1row[:1, tok0 - 1 + lo:tok0 - 1 + hi],
                start=True, stop=True), s_t)
            tcnt = s_t.n
            em.wait("vector", s_t, tcnt)
            em.op("vector", lambda V, sl=sl, lo=lo, hi=hi: V.tensor_copy(
                sl[:, lo:hi], pbc[:, 0:hi - lo]), s_v)
            pbc_free[0] = s_v.n
            rs1_rdy[j] = s_v.n

        def stage_A1(i):
            ip = i % 2
            W = xwv[ip]
            SY = sy2v[ip]
            mxps = mxp[:, ip * TW:ip * TW + TW]
            mxns = mxn[:, ip * TW:ip * TW + TW]
            rsl = rs1b[:, ip * TWW:ip * TWW + TWW]
            lo, hi = _edges(i)
            if i >= 2:
                em.wait("vector", s_t, tp_done[i - 2])
            em.op("vector", lambda V: V.memset(rmx[:], -1e30), s_v)
            em.op("vector", lambda V: V.memset(rmn[:], 1e30), s_v)
            edge_done = 0
            em.wait("vector", s_ld, ld_xw[i])
            if lo == 1:
                em.op("vector", lambda V, W=W: V.memset(W[:, :, 0:1], 0.0), s_v)
                edge_done = s_v.n
            if hi == TWW - 1:
                em.op("vector", lambda V, W=W: V.memset(
                    W[:, :, TWW - 1:TWW], 0.0), s_v)
                edge_done = s_v.n
            em.wait("tensor", s_ld, ld_xw[i])
            if edge_done:
                em.wait("tensor", s_v, edge_done)
            for cc in range(NCH):
                pA = fAs[cc % 2]
                hv = hsc[:, cc * TWW:(cc + 1) * TWW]
                em.wait("tensor", s_s, fa_evac[cc % 2])
                for c in range(NC8):
                    em.op("tensor", lambda T, pA=pA, c=c, cc=cc, W=W: T.matmul(
                        pA[:], t1v[:, c, cc * 128:(cc + 1) * 128], W[:, c, :],
                        start=(c == 0), stop=(c == NC8 - 1)), s_t)
                mm_cnt = s_t.n
                em.wait("scalar", s_t, mm_cnt)
                em.wait("scalar", s_v, hsc_y1v[cc])
                em.wait("scalar", s_g, hsc_desc[cc])
                em.op("scalar", lambda Sc, pA=pA, hv=hv: Sc.activation(
                    hv, pA[:], AF.Identity, scale=w1r_c[:, 0:1]), s_s)
                fa_evac[cc % 2] = s_s.n
                em.wait("gpsimd", s_s, fa_evac[cc % 2])
                em.wait("gpsimd", s_v, rs1_rdy[i])
                em.op("gpsimd", lambda G, hv=hv, rsl=rsl: G.tensor_mul(
                    hv, hv, rsl), s_g)
                hsc_desc[cc] = s_g.n
            ffn1_done[i] = s_t.n
            ds_last[i] = s_g.n
            if i + 1 < NTF:
                emit_xw_load(i + 1)
                build_rs1b(i + 1)

        def stage_A2(i):
            ip = i % 2
            SY = sy2v[ip]
            mxps = mxp[:, ip * TW:ip * TW + TW]
            mxns = mxn[:, ip * TW:ip * TW + TW]
            if i >= 2:
                em.wait("scalar", s_t, ffn2_done[i - 2])
            def y0_emit(cc):
                hv = hsc[:, cc * TWW:(cc + 1) * TWW]
                y0sl = y0s[:, (cc % 2) * TW:(cc % 2) * TW + TW]
                em.wait("scalar", s_g, hsc_desc[cc])
                em.wait("scalar", s_v, y0_read[cc % 2])
                em.op("scalar", lambda Sc, hv=hv, cc=cc, y0sl=y0sl: Sc.activation(
                    y0sl, hv[:, 1:TW + 1], AF.Identity,
                    bias=cb_c[:, cc:cc + 1],
                    scale=cw_c[:, cc * 3 + 1:cc * 3 + 2]), s_s)
                return s_s.n

            y0_cnt = [0] * NCH
            y0_cnt[0] = y0_emit(0)
            for cc in range(NCH):
                hv = hsc[:, cc * TWW:(cc + 1) * TWW]
                y1sl = y1s[:, (cc % 2) * TW:(cc % 2) * TW + TW]
                em.wait("vector", s_s, y0_cnt[cc])
                em.wait("vector", s_s, y1_read[cc % 2])
                em.op("vector", lambda V, hv=hv, cc=cc, y1sl=y1sl:
                      V.scalar_tensor_tensor(
                          y1sl, hv[:, 0:TW], cwb_c[:, cc * 3:cc * 3 + 1],
                          y0s[:, (cc % 2) * TW:(cc % 2) * TW + TW],
                          ALU.mult, ALU.add), s_v)
                y0_read[cc % 2] = s_v.n
                em.op("vector", lambda V, hv=hv, cc=cc, y1sl=y1sl:
                      V.scalar_tensor_tensor(
                          y1sl, hv[:, 2:TW + 2], cwb_c[:, cc * 3 + 2:cc * 3 + 3],
                          y1sl, ALU.mult, ALU.add), s_v)
                hsc_y1v[cc] = s_v.n
                y1b_cnt = s_v.n
                if cc + 1 < NCH:
                    y0_cnt[cc + 1] = y0_emit(cc + 1)
                em.wait("scalar", s_v, y1b_cnt)
                em.op("scalar", lambda Sc, SY=SY, cc=cc, y1sl=y1sl: Sc.activation(
                    SY[:, cc, :], y1sl, AF.Silu), s_s)
                y1_read[cc % 2] = s_s.n
                silu_cnt = s_s.n
                if cc % 2 == 1:
                    SYp = sy2[:, ip * NCH * TW + (cc - 1) * TW:
                              ip * NCH * TW + (cc + 1) * TW]
                    em.wait("vector", s_s, silu_cnt)
                    em.op("vector", lambda V, SYp=SYp: V.tensor_max(
                        rmx[:], rmx[:], SYp), s_v)
                    em.op("vector", lambda V, SYp=SYp: V.tensor_tensor(
                        rmn[:], rmn[:], SYp, ALU.min), s_v)
            mm_g = s_g.n
            rx3 = rmx[:].rearrange("p (l x) -> p l x", l=2)
            rn3 = rmn[:].rearrange("p (l x) -> p l x", l=2)
            em.op("vector", lambda V, mxps=mxps: V.tensor_tensor(
                mxps, rx3[:, 0, :], rx3[:, 1, :], ALU.max), s_v)
            em.op("vector", lambda V, mxns=mxns: V.tensor_tensor(
                mxns, rn3[:, 0, :], rn3[:, 1, :], ALU.min), s_v)
            minmax_done[i] = s_v.n

        def stage_B(i):
            ip = i % 2
            mxps = mxp[:, ip * TW:ip * TW + TW]
            mxns = mxn[:, ip * TW:ip * TW + TW]
            axs = amx[:, ip * 2:ip * 2 + 2]
            rcs = rc2[:, ip * 2:ip * 2 + 2]
            em.opw("vector", lambda V, mxns=mxns: V.tensor_scalar(
                mxns, mxns, -1.0, None, ALU.mult), s_v)
            em.opw("vector", lambda V, mxps=mxps, mxns=mxns: V.tensor_max(
                mxps, mxps, mxns), s_v)
            vcnt = s_v.n
            em.wait("tensor", s_v, vcnt)
            em.wait("tensor", s_v, ptrq_read[0])
            for t in range(2):
                em.op("tensor", lambda T, t=t, mxps=mxps: T.transpose(
                    ptr_q[:, t * 128:(t + 1) * 128],
                    mxps[:, t * 128:(t + 1) * 128], identb[:]), s_t)
            tp_done[i] = s_t.n
            em.wait("vector", s_t, tp_done[i])
            for t in range(2):
                em.opw("vector", lambda V, t=t, axs=axs: V.reduce_max(
                    axs[:, t:t + 1], ptr_q[:, t * 128:(t + 1) * 128], AX.X), s_v)
            ptrq_read[0] = s_v.n
            em.opw("vector", lambda V, axs=axs: V.tensor_scalar(
                axs, axs, 1e-5, None, ALU.max), s_v)
            em.opw("vector", lambda V, axs=axs, rcs=rcs: V.reciprocal(
                rcs, axs), s_v)
            em.opw("vector", lambda V, rcs=rcs: V.tensor_scalar(
                rcs, rcs, 127.0, None, ALU.mult), s_v)
            vcnt = s_v.n
            em.wait("tensor", s_v, vcnt)
            em.wait("tensor", s_v, pbc_free[0])
            for t in range(2):
                em.op("tensor", lambda T, t=t, rcs=rcs: T.transpose(
                    pbc[:1, t * 128:(t + 1) * 128], rcs[:, t:t + 1],
                    identf[:]), s_t)
            tcnt = s_t.n
            em.wait("vector", s_t, tcnt)
            em.op("vector", lambda V: V.tensor_copy(
                s2row[:1, :], pbc[:1, 0:TW]), s_v)
            srow_cnt = s_v.n
            em.wait("tensor", s_v, srow_cnt)
            em.op("tensor", lambda T: T.matmul(
                pbc[:, 0:TW], onesrf[:1, :], s2row[:1, :],
                start=True, stop=True), s_t)
            tcnt = s_t.n
            em.wait("vector", s_t, tcnt)
            em.op("vector", lambda V: V.tensor_copy(s2b[:], pbc[:, 0:TW]), s_v)
            pbc_free[0] = s_v.n

        def stage_C(i):
            ip = i % 2
            em.op("vector", lambda V: V.tensor_copy(s2bw[:, 0:TW], s2b[:]), s_v)
            em.op("vector", lambda V: V.tensor_copy(s2bw[:, TW:2 * TW], s2b[:]), s_v)
            for cp in range(NCH // 2):
                par = cp % 2
                tq = tqP[par][:]; nq = nqP[par][:]
                dq = dqP[par][:]; ea = eaP[par][:]
                SYp = sy2[:, ip * NCH * TW + cp * 2 * TW:
                          ip * NCH * TW + (cp + 1) * 2 * TW]
                em.op("vector", lambda V, tq=tq, SYp=SYp: V.tensor_mul(
                    tq, SYp, s2bw[:]), s_v)
                em.op("vector", lambda V, tq=tq, nq=nq: V.tensor_copy(nq, tq), s_v)
                em.op("vector", lambda V, tq=tq, nq=nq, dq=dq: V.tensor_sub(
                    dq, tq, nq), s_v)
                em.op("vector", lambda V, dq=dq, ea=ea: V.tensor_scalar(
                    ea, dq, -0.5, None, ALU.is_lt), s_v)
                for thr, op0 in ((0.5, ALU.is_gt), (-1.5, ALU.is_lt),
                                 (1.5, ALU.is_gt), (-2.5, ALU.is_lt),
                                 (2.5, ALU.is_gt)):
                    em.op("vector", lambda V, dq=dq, ea=ea, thr=thr, op0=op0:
                          V.scalar_tensor_tensor(
                              ea, dq, thr, ea, op0, ALU.subtract), s_v)
                em.op("vector", lambda V, SYp=SYp, ea=ea, nq=nq:
                      V.tensor_add(SYp, ea, nq), s_v)
                g_done[par] = s_v.n
            q2_all[i] = s_v.n

        def stage_D1(i, mid=None):
            ip = i % 2
            SY = sy2v[ip]
            em.wait("tensor", s_v, q2_all[i])
            for tt in range(2):
                em.wait("vector", s_s, col_read[tt])
                em.op("vector", lambda V, tt=tt, ip=ip: V.scalar_tensor_tensor(
                    col(32 + tt), amx[:, ip * 2 + tt:ip * 2 + tt + 1],
                    1.0 / 127.0, w2r_c[:, 0:1], ALU.mult, ALU.mult), s_v)
                rc_done = s_v.n
                for half in range(2):
                    bk = tt * 2 + half
                    pb = f2s[bk]
                    em.wait("tensor", s_s, f2_evac[bk])
                    for cc in range(NCH):
                        em.op("tensor", lambda T, pb=pb, cc=cc, tt=tt, SY=SY,
                              half=half: T.matmul(
                                  pb[:], SY[:, cc, tt * 128:(tt + 1) * 128],
                                  t2v[:, cc, half * 512:(half + 1) * 512],
                                  start=(cc == 0), stop=(cc == NCH - 1)), s_t)
                    f2_mm[bk] = s_t.n
                    em.wait("scalar", s_t, f2_mm[bk])
                    em.wait("scalar", s_v, rc_done)
                    em.wait("scalar", s_v, o2_vread[ip])
                    em.op("scalar", lambda Sc, pb=pb, tt=tt, ip=ip, half=half:
                          Sc.activation(
                              o2[:, ip * 2 * D + tt * D + half * 512:
                                 ip * 2 * D + tt * D + half * 512 + 512],
                              pb[:], AF.Identity, scale=col(32 + tt)), s_s)
                    f2_evac[bk] = s_s.n
                    col_read[tt] = s_s.n
                    if bk == 2 and mid is not None:
                        mid()
            ffn2_done[i] = s_t.n
            o2_rdy[i] = s_s.n

        def stage_D2(i):
            ip = i % 2
            bb = (i * TW) // S
            s0 = (i * TW) % S
            em.wait("sync", s_v, x1_read[0])
            em.dma("sync", x1F[:].rearrange("p (t d) -> p t d", t=2),
                   x1_h[bb, s0:s0 + TW, :].rearrange(
                       "(t p) d -> p t d", p=128), s_ld)
            x1_ld = s_ld.n
            o2p = o2[:, ip * 2 * D:(ip + 1) * 2 * D]
            o3 = o2p.rearrange("p (t d) -> p t d", t=2)
            cS = colp[:, 16:18]
            cQ = colp[:, 18:20]
            cMu = colp[:, 20:22]
            cM2 = colp[:, 22:24]
            cVar = colp[:, 24:26]
            cSd = colp[:, 26:28]
            cRs = colp[:, 28:30]
            cB = colp[:, 30:32]
            ln3 = lnsq[:].rearrange("p (t d) -> p t d", t=2)
            em.wait("scalar", s_s, o2_rdy[i])
            em.wait("scalar", s_v, aF_read[0])
            em.op("scalar", lambda Sc, o2p=o2p: Sc.activation(
                lnsq[:], o2p, AF.Square), s_s)
            sq_cnt = s_s.n
            em.wait("vector", s_s, o2_rdy[i])
            em.opw("vector", lambda V, cS=cS, o3=o3: V.reduce_sum(
                cS.unsqueeze(2), o3, AX.X), s_v)
            o2_vread[ip] = s_v.n
            em.wait("vector", s_s, sq_cnt)
            em.opw("vector", lambda V, cQ=cQ, ln3=ln3: V.reduce_sum(
                cQ.unsqueeze(2), ln3, AX.X), s_v)
            em.opw("vector", lambda V, cMu=cMu, cS=cS: V.tensor_scalar(
                cMu, cS, 1.0 / D, None, ALU.mult), s_v)
            em.opw("vector", lambda V, cM2=cM2, cMu=cMu: V.tensor_mul(
                cM2, cMu, cMu), s_v)
            em.opw("vector", lambda V, cVar=cVar, cQ=cQ, cM2=cM2:
                  V.scalar_tensor_tensor(
                      cVar, cQ, 1.0 / D, cM2, ALU.mult, ALU.subtract), s_v)
            em.opw("vector", lambda V, cVar=cVar: V.tensor_scalar(
                cVar, cVar, 1e-5, None, ALU.add), s_v)
            vc = s_v.n
            em.wait("scalar", s_v, vc)
            em.op("scalar", lambda Sc, cSd=cSd, cVar=cVar: Sc.activation(
                cSd, cVar, AF.Sqrt), s_s)
            sd_cnt = s_s.n
            em.wait("vector", s_s, sd_cnt)
            em.opw("vector", lambda V, cRs=cRs, cSd=cSd: V.reciprocal(
                cRs, cSd), s_v)
            em.opw("vector", lambda V, cB=cB, cMu=cMu, cRs=cRs:
                  V.scalar_tensor_tensor(
                      cB, cMu, -1.0, cRs, ALU.mult, ALU.mult), s_v)
            vc = s_v.n
            em.wait("scalar", s_v, vc)
            for tt in range(2):
                em.op("scalar", lambda Sc, tt=tt, o3=o3, cB=cB, cRs=cRs:
                      Sc.activation(
                          aF[:, tt * D:tt * D + D], o3[:, tt, :], AF.Identity,
                          bias=cB[:, tt:tt + 1], scale=cRs[:, tt:tt + 1]), s_s)
            ln_cnt = s_s.n
            em.wait("vector", s_s, ln_cnt)
            em.wait("vector", s_ld, x1_ld)
            em.wait("vector", s_st, xo_dma[0])
            em.op("vector", lambda V: V.tensor_add(xoF[:], aF[:], x1F[:]), s_v)
            aF_read[0] = s_v.n
            x1_read[0] = s_v.n
            addc = s_v.n
            em.wait("gpsimd", s_v, addc)
            em.dma("gpsimd", out_d[bb, s0:s0 + TW, :].rearrange(
                "(t p) d -> p t d", p=128),
                   xoF[:].rearrange("p (t d) -> p t d", t=2), s_st)
            xo_dma[0] = s_st.n

        emit_xw_load(0)
        build_rs1b(0)
        stage_A1(0)
        stage_A2(0)
        stage_B(0)
        for i in range(1, NTF):
            stage_A1(i)
            stage_C(i - 1)
            stage_A2(i)
            if i >= 2:
                stage_D2(i - 2)
            stage_D1(i - 1, mid=lambda i=i: stage_B(i))
        stage_C(NTF - 1)
        stage_D1(NTF - 1)
        stage_D2(NTF - 2)
        stage_D2(NTF - 1)

    es_f1.close()
    es_ff.close()
    em.wait("gpsimd", s_st)

    with nc.Block() as block:
        block.sync(lambda EE: [f(EE) for f in em.q["sync"]])
        block.tensor(lambda EE: [f(EE) for f in em.q["tensor"]])
        block.vector(lambda EE: [f(EE) for f in em.q["vector"]])
        block.scalar(lambda EE: [f(EE) for f in em.q["scalar"]])
        block.gpsimd(lambda EE: [f(EE) for f in em.q["gpsimd"]])
    es.close()
    return nc


def kernel(L0_attn_x, L0_attn_wq, L0_attn_bq, L0_attn_wk, L0_attn_bk,
           L0_attn_wv, L0_attn_bv, L0_attn_wo, L0_attn_bo, L0_ffn_w1,
           L0_ffn_conv_w, L0_ffn_conv_b, L0_ffn_w2):
    x = np.ascontiguousarray(np.asarray(L0_attn_x, dtype=np.float32))
    w = {k: np.ascontiguousarray(np.asarray(v, dtype=np.float32)) for k, v in [
        ("wq", L0_attn_wq), ("bq", L0_attn_bq), ("wk", L0_attn_wk),
        ("bk", L0_attn_bk), ("wv", L0_attn_wv), ("bv", L0_attn_bv),
        ("wo", L0_attn_wo), ("bo", L0_attn_bo), ("w1", L0_ffn_w1),
        ("cw", L0_ffn_conv_w), ("cb", L0_ffn_conv_b), ("w2", L0_ffn_w2)]}
    nc = bass.Bass("TRN2", target_bir_lowering=False,
                   detect_race_conditions=False)
    build(nc)
    in_maps = [dict(x=x[2 * i:2 * i + 2], **w) for i in range(8)]
    res = run_bass_kernel_spmd(nc, in_maps, core_ids=list(range(8)))
    return np.concatenate([r["out"] for r in res.results], axis=0)
